# version 04
# baseline (speedup 1.0000x reference)
import numpy as np
from contextlib import ExitStack

import concourse.bass as bass
import concourse.mybir as mybir
from concourse.bass import IndirectOffsetOnAxis
from concourse.bass_utils import run_bass_kernel_spmd

F32 = mybir.dt.float32
BF16 = mybir.dt.bfloat16
U32 = mybir.dt.uint32
AF = mybir.ActivationFunctionType
ALU = mybir.AluOpType
AX = mybir.AxisListType.X

NCORE = 8
BATCH, SEQ, D = 32, 2048, 1024
BLOC = BATCH // NCORE
TOK = BLOC * SEQ
NT = TOK // 128
DCH = D // 128
CTXB, CTXS, CD = 2, 2048, 768
CTXT = CTXB * CTXS
CCH = CD // 128
HEADS, DH = 8, 128
SCALE_Q = 1.5 * DH ** -0.5
NEXP, CAP = 16, 4096
HID = 4096
EXP_PC = NEXP // NCORE
SLOTS = EXP_PC * CAP
CIN = 2048
DST = 64
TAPS = 24
EPS = 1e-5
BIGF = 1.0e9


class P(object):

    def __init__(self, nc, stack, names):
        self.nc = nc
        self.sem = {}
        self.cnt = {}
        self.waited = {}
        for n in names:
            self.sem[n] = stack.enter_context(nc.semaphore("sem_" + n))
            self.cnt[n] = 0

    def inc(self, key, instr, n=1):
        instr.then_inc(self.sem[key], n)
        self.cnt[key] += n
        return self.cnt[key]

    def tok(self, key):
        return self.cnt[key]

    def wait(self, eng, key, target=None):
        if target is None:
            target = self.cnt[key]
        if target <= 0:
            return
        k = (id(eng), key)
        if target > self.waited.get(k, 0):
            eng.wait_ge(self.sem[key], target)
            self.waited[k] = target

    def dma(self, eng, key, out, in_, **kw):
        return self.inc(key, eng.dma_start(out=out, in_=in_, **kw), 16)


def build_L1(nqt=None, stage=99):
    nc = bass.Bass("TRN2", target_bir_lowering=False, debug=False,
                   detect_race_conditions=False)
    st = ExitStack()
    x_in = nc.dram_tensor("x", [TOK, D], F32, kind="ExternalInput").ap()
    ctx_in = nc.dram_tensor("ctx", [CTXT, CD], F32, kind="ExternalInput").ap()
    wq_in = nc.dram_tensor("wq", [D, D], F32, kind="ExternalInput").ap()
    wk_in = nc.dram_tensor("wk", [CD, D], F32, kind="ExternalInput").ap()
    wv_in = nc.dram_tensor("wv", [CD, D], F32, kind="ExternalInput").ap()
    wo_in = nc.dram_tensor("wo", [D, D], F32, kind="ExternalInput").ap()
    bout_in = nc.dram_tensor("bout", [1, D], F32, kind="ExternalInput").ap()
    wg_in = nc.dram_tensor("wg", [D, NEXP], F32, kind="ExternalInput").ap()
    bg_in = nc.dram_tensor("bg", [1, NEXP], F32, kind="ExternalInput").ap()

    s1_out = nc.dram_tensor("s1", [TOK, D], F32, kind="ExternalOutput").ap()
    rows_out = nc.dram_tensor("rows", [TOK, D], BF16, kind="ExternalOutput").ap()
    idx_out = nc.dram_tensor("idx", [128, NT], F32, kind="ExternalOutput").ap()
    gate_out = nc.dram_tensor("gate", [128, NT], F32, kind="ExternalOutput").ap()
    lpos_out = nc.dram_tensor("lpos", [128, NT], F32, kind="ExternalOutput").ap()
    cnt_out = nc.dram_tensor("cnt", [1, NEXP], F32, kind="ExternalOutput").ap()

    lo_rows = nc.dram_tensor("lo_rows", [TOK, D], BF16, kind="Internal").ap()
    lnx_rows = nc.dram_tensor("lnx_rows", [TOK, D], BF16, kind="Internal").ap()
    v_dram = nc.dram_tensor("v_dram", [CTXT, D], BF16, kind="Internal").ap()
    ctx_bf = nc.dram_tensor("ctx_bf", [CTXT, CD], BF16, kind="Internal").ap()

    te, ve, ac, gp, sy = nc.tensor, nc.vector, nc.scalar, nc.gpsimd, nc.sync

    with st:
        p = P(nc, st, ("gp", "te", "ve", "ac", "ldA", "ldB", "ldG", "stR", "trA", "dbV"))
        sb = lambda name, shape, dt: st.enter_context(nc.sbuf_tensor(name, shape, dt))
        ps = lambda name, shape: st.enter_context(nc.psum_tensor(name, shape, F32))

        k_fm = sb("k_fm", [128, HEADS, CTXT], BF16)
        wq_sb = sb("wq_sb", [128, DCH, D], BF16)
        wo_sb = sb("wo_sb", [128, HEADS, D], BF16)
        wg_hi = sb("wg_hi", [128, DCH, NEXP], BF16)
        wg_lo = sb("wg_lo", [128, DCH, NEXP], BF16)
        boutb = sb("boutb", [128, D], F32)
        bg_row = sb("bg_row", [1, NEXP], F32)
        ones_r = sb("ones_r", [1, 128], F32)
        ones_k = sb("ones_k", [128, 1], F32)
        trilbf = sb("trilbf", [128, 128], BF16)
        iota_e = sb("iota_e", [128, NEXP], F32)
        iota_me = sb("iota_me", [128, NEXP], F32)
        run_tot = sb("run_tot", [1, NEXP], F32)
        mcol = sb("mcol", [128, 1], F32)
        eps_col = sb("eps_col", [128, 1], F32)
        sqs = sb("sqs", [128, 1], F32)
        vcol = sb("vcol", [128, 1], F32)
        bgb = sb("bgb", [128, NEXP], F32)
        runb = sb("runb", [128, NEXP], F32)
        idx_acc = sb("idx_acc", [128, NT], F32)
        gate_acc = sb("gate_acc", [128, NT], F32)
        lpos_acc = sb("lpos_acc", [128, NT], F32)

        logit_t = sb("logit_t", [128, 32], F32)
        mx_t = sb("mx_t", [128, 1], F32)
        nmx_t = sb("nmx_t", [128, 1], F32)
        ssum_t = sb("ssum_t", [128, 1], F32)
        gate_col = sb("gate_col", [128, 1], F32)
        eq_t = sb("eq_t", [128, NEXP], F32)
        cand_t = sb("cand_t", [128, 32], F32)
        idx_col = sb("idx_col", [128, 1], F32)
        oh_t = sb("oh_t", [128, NEXP], BF16)
        ohf_t = sb("ohf_t", [128, NEXP], F32)
        gsel_t = sb("gsel_t", [128, 32], F32)
        lpos_col = sb("lpos_col", [128, 1], F32)

        pA = ps("pA", [128, 512])
        pB = ps("pB", [128, 512])
        pC = ps("pC", [128, 512])
        pD = ps("pD", [128, 512])
        pE = ps("pE", [128, 512])
        pF = ps("pF", [128, 512])
        pG = ps("pG", [128, 512])
        pR = ps("pR", [128, 512])

        tr = ExitStack()
        sbt = lambda name, shape, dt: tr.enter_context(nc.sbuf_tensor(name, shape, dt))
        wk_sb = sbt("wk_sb", [128, CCH, D], BF16)
        wv_sb = sbt("wv_sb", [128, CCH, D], BF16)
        ctx_fm = sbt("ctx_fm", [128, CCH, CTXT], BF16)
        v_t = sbt("v_t", [128, D], BF16)
        bout_sb = sbt("bout_sb", [1, D], F32)
        wg32 = sbt("wg32", [128, DCH, NEXP], F32)
        wg_t32 = sbt("wg_t32", [128, DCH, NEXP], F32)
        tril_f = sbt("tril_f", [128, 128], F32)
        iota_rf = sbt("iota_rf", [128, 128], F32)

        gp.iota(iota_e[:], pattern=[[1, NEXP]], base=0, channel_multiplier=0,
                allow_small_or_imprecise_dtypes=True)
        gp.iota(iota_rf[:], pattern=[[1, 128]], base=0, channel_multiplier=0,
                allow_small_or_imprecise_dtypes=True)
        c_iota = p.inc("gp", gp.iota(mcol[:], pattern=[[0, 1]], base=0,
                                     channel_multiplier=1,
                                     allow_small_or_imprecise_dtypes=True))
        p.wait(ve, "gp", c_iota)
        ve.memset(ones_r[:], 1.0)
        ve.memset(eps_col[:], EPS)
        ve.memset(logit_t[:, 16:32], -1.0e30)
        ve.memset(cand_t[:, 16:32], 1.0e30)
        ve.memset(gsel_t[:, 16:32], 0.0)
        ve.memset(ones_k[:], 1.0)
        ve.memset(run_tot[:], 0.0)
        ve.tensor_scalar_add(iota_me[:], iota_e[:], -65536.0)
        ve.tensor_scalar(tril_f[:], iota_rf[:], mcol[:, 0:1], None, op0=ALU.is_gt)
        ve.drain()
        c_tril = p.inc("ve", ve.tensor_copy(trilbf[:], tril_f[:]))

        for j in range(CCH):
            p.dma(gp, "ldG", wk_sb[:, j, :], wk_in[j * 128:(j + 1) * 128, :])
            p.dma(gp, "ldG", wv_sb[:, j, :], wv_in[j * 128:(j + 1) * 128, :])
        p.dma(gp, "ldG", ctx_bf[:, :], ctx_in[:, :])
        t_wkv = p.tok("ldG")
        for j in range(DCH):
            p.dma(gp, "ldG", wq_sb[:, j, :], wq_in[j * 128:(j + 1) * 128, :])
        for h in range(HEADS):
            p.dma(gp, "ldG", wo_sb[:, h, :], wo_in[h * 128:(h + 1) * 128, :])
        t_wq = p.tok("ldG")
        p.dma(sy, "ldA", bout_sb[:], bout_in)
        p.dma(sy, "ldA", bg_row[:], bg_in)
        for j in range(DCH):
            p.dma(sy, "ldA", wg32[:, j, :], wg_in[j * 128:(j + 1) * 128, :])
        t_misc = p.tok("ldA")
        p.wait(ve, "ldA", t_misc)
        ve.tensor_copy(wg_hi[:, :, :], wg32[:, :, :])
        ve.drain()
        ve.tensor_copy(wg_t32[:, :, :], wg_hi[:, :, :])
        ve.drain()
        ve.tensor_sub(wg_t32[:, :, :], wg32[:, :, :], wg_t32[:, :, :])
        ve.drain()
        c_wg = p.inc("ve", ve.tensor_copy(wg_lo[:, :, :], wg_t32[:, :, :]))

        if stage < 2:
            p.wait(sy, "stR")
            p.wait(sy, "ldG")
            p.wait(sy, "ldA")
            p.wait(sy, "ve", p.tok("ve"))
            tr.close()
            return nc
        p.wait(sy, "ldG", t_wkv)
        for j in range(CCH):
            p.dma(sy, "trA", ctx_fm[:, j, :], ctx_bf[:, j * 128:(j + 1) * 128],
                  transpose=True)
        t_ctxfm = p.tok("trA")

        p.wait(te, "ldA", t_misc)
        p.wait(te, "ve", c_tril)
        ev_bout = []
        for half in range(2):
            cte = p.inc("te", te.matmul(pR[:, :], ones_r[:, :],
                                        bout_sb[:, half * 512:(half + 1) * 512],
                                        start=True, stop=True))
            p.wait(ac, "te", cte)
            cac = p.inc("ac", ac.copy(boutb[:, half * 512:(half + 1) * 512], pR[:, :]))
            ev_bout.append(cac)
            p.wait(te, "ac", cac)
        c_boutb = ev_bout[-1]
        cte = p.inc("te", te.matmul(pR[:, 0:NEXP], ones_r[:, :], bg_row[:, :],
                                    start=True, stop=True))
        p.wait(ac, "te", cte)
        c_bgb = p.inc("ac", ac.copy(bgb[:, :], pR[:, 0:NEXP]))
        p.wait(te, "ac", c_bgb)

        if stage < 3:
            p.wait(sy, "trA")
            p.wait(sy, "ac", p.tok("ac"))
            tr.close()
            return nc
        p.wait(te, "trA", t_ctxfm)
        ev_ac_k = {}
        nK = 0
        for h in range(HEADS):
            for kt in range(CTXT // 512):
                pb = pA if (nK % 2 == 0) else pB
                if nK >= 2:
                    p.wait(te, "ac", ev_ac_k[nK - 2])
                for j in range(CCH):
                    mi = te.matmul(pb[:, :], wk_sb[:, j, h * 128:(h + 1) * 128],
                                   ctx_fm[:, j, kt * 512:(kt + 1) * 512],
                                   start=(j == 0), stop=(j == CCH - 1))
                cte = p.inc("te", mi)
                p.wait(ac, "te", cte)
                ev_ac_k[nK] = p.inc("ac", ac.copy(k_fm[:, h, kt * 512:(kt + 1) * 512],
                                                  pb[:, :]))
                nK += 1

        if stage < 4:
            p.wait(sy, "ac", p.tok("ac"))
            tr.close()
            return nc
        ev_st_v = {}
        for kt in range(CTXT // 128):
            if kt >= 1:
                p.wait(ac, "stR", ev_st_v[kt - 1])
            for half in range(2):
                pb = pC if half == 0 else pD
                for j in range(CCH):
                    mi = te.matmul(pb[:, :], ctx_fm[:, j, kt * 128:(kt + 1) * 128],
                                   wv_sb[:, j, half * 512:(half + 1) * 512],
                                   start=(j == 0), stop=(j == CCH - 1))
            cte = p.inc("te", mi)
            p.wait(ac, "te", cte)
            ac.copy(v_t[:, 0:512], pC[:, :])
            cac = p.inc("ac", ac.copy(v_t[:, 512:1024], pD[:, :]))
            p.wait(te, "ac", cac)
            p.wait(sy, "ac", cac)
            ev_st_v[kt] = p.dma(sy, "stR", v_dram[kt * 128:(kt + 1) * 128, :], v_t[:])
        t_vdone = p.tok("stR")

        tr.close()
        for eng in (ve, ac, te, gp, sy):
            for key in ("trA", "stR", "ldA", "ldG", "te", "ve", "ac", "gp"):
                p.wait(eng, key)
        x_t = [sb("x_t%d" % i, [128, D], F32) for i in range(2)]
        tmp_t = sb("tmp_t", [128, D], F32)
        lnbf_t = sb("lnbf_t", [128, D], BF16)
        lnx_fm = sb("lnx_fm", [128, DCH, 1024], BF16)
        q_fm = sb("q_fm", [128, HEADS, 1024], BF16)
        v_buf = [sb("v_buf%d" % i, [128, 32, DH], BF16) for i in range(2)]
        exps = [sb("exps%d" % i, [128, 1024], BF16) for i in range(2)]
        densum = sb("densum", [128, 1024], F32)
        den_row = sb("den_row", [1, 1024], F32)
        rdenb = sb("rdenb", [128, 1024], F32)
        attn_fm = sb("attn_fm", [128, HEADS, 1024], BF16)
        s1_t = sb("s1_t", [128, D], F32)
        hi32_t = sb("hi32_t", [128, D], F32)
        lo_t = sb("lo_t", [128, D], BF16)
        hi_fm = lnx_fm
        lo_fm = q_fm
        p.inc("ve", ve.memset(densum[:], 0.0))

        QT = 1024
        NQT = TOK // QT if nqt is None else nqt
        ev_ve_attn = 0
        cac_rdenb = 0
        ev_evac = {0: 0, 1: 0}
        ev_pv_prev = 0
        tile_i = 0
        ev_ve_router_prev = p.tok("ve")
        ev_te_qtile = p.tok("te")
        p.wait(sy, "stR", t_vdone)
        p.wait(sy, "te", ev_te_qtile)
        p.wait(sy, "ve", ev_ve_router_prev)
        for qi in range(NQT):
            q0 = qi * QT
            p.wait(sy, "ve", ev_ve_router_prev)
            for tt in range(QT // 128):
                r0 = q0 + tt * 128
                xt = x_t[tt % 2]
                tx = p.dma(sy, "ldA", xt[:], x_in[r0:r0 + 128, :])
                p.wait(ve, "ldA", tx)
                p.wait(ve, "stR")
                ve.reduce_sum(sqs[:], xt[:], axis=AX, negate=True)
                ve.drain()
                ve.tensor_scalar_mul(mcol[:], sqs[:], 1.0 / D)
                ve.drain()
                cve = p.inc("ve", ve.tensor_scalar_add(tmp_t[:], xt[:], mcol[:, 0:1]))
                p.wait(ac, "ve", cve)
                ac.activation(hi32_t[:], tmp_t[:], AF.Square, accum_out=sqs[:, 0:1])
                ac.drain()
                ac.activation(vcol[:], sqs[:, 0:1], AF.Ln, scale=1.0 / D, bias=eps_col[:, 0:1])
                ac.drain()
                ac.activation(vcol[:], vcol[:], AF.Exp, scale=-0.5)
                cac = p.inc("ac", ac.drain())
                p.wait(ve, "ac", cac)
                ve.tensor_scalar(lnbf_t[:], tmp_t[:], vcol[:, 0:1],
                                 None, op0=ALU.mult)
                cve = p.inc("ve", ve.drain())
                p.wait(sy, "ve", cve)
                p.dma(sy, "stR", lnx_rows[r0:r0 + 128, :], lnbf_t[:])
            t_lnq = p.tok("stR")
            p.wait(sy, "stR", t_lnq)
            p.wait(sy, "te", ev_te_qtile)
            for j in range(DCH):
                p.dma(sy, "trA", lnx_fm[:, j, :],
                      lnx_rows[q0:q0 + QT, j * 128:(j + 1) * 128], transpose=True)
            t_fmq = p.tok("trA")
            p.wait(te, "trA", t_fmq)
            p.wait(te, "ldG", t_wq)
            p.wait(te, "ve", ev_ve_attn)
            for h in range(HEADS):
                for half in range(2):
                    pb = pA if half == 0 else pB
                    for j in range(DCH):
                        mi = te.matmul(pb[:, :], wq_sb[:, j, h * 128:(h + 1) * 128],
                                       lnx_fm[:, j, half * 512:(half + 1) * 512],
                                       start=(j == 0), stop=(j == DCH - 1))
                cte = p.inc("te", mi)
                p.wait(ac, "te", cte)
                ac.activation(q_fm[:, h, 0:512], pA[:, :], AF.Copy, scale=SCALE_Q)
                cac = p.inc("ac", ac.activation(q_fm[:, h, 512:1024], pB[:, :],
                                                AF.Copy, scale=SCALE_Q))
                p.wait(te, "ac", cac)
            p.wait(gp, "stR", t_vdone)
            p.wait(gp, "te", ev_pv_prev)
            for kc in range(32):
                p.dma(gp, "dbV", v_buf[0][:, kc, :],
                      v_dram[kc * 128:(kc + 1) * 128, 0:DH])
            ev_pv_head = {}
            for h in range(HEADS):
                t_vh = p.tok("dbV")
                if h + 1 < HEADS:
                    if h >= 1:
                        p.wait(gp, "te", ev_pv_head[h - 1])
                    for kc in range(32):
                        p.dma(gp, "dbV", v_buf[(h + 1) % 2][:, kc, :],
                              v_dram[kc * 128:(kc + 1) * 128, (h + 1) * DH:(h + 2) * DH])
                vb = v_buf[h % 2]
                dsum = densum
                pPV0, pPV1 = (pE, pF) if h % 2 == 0 else (pG, pR)
                p.wait(te, "dbV", t_vh)
                cacs, cpvs, cves = [], [], []

                def qk_banks(kc):
                    if kc < 4:
                        return pB, pD
                    return (pA, pC) if kc % 2 == 0 else (pB, pD)

                def emit_pv(kc):
                    exb = exps[kc % 2]
                    p.wait(te, "ac", cacs[kc])
                    if kc == 0:
                        p.wait(te, "ve", ev_evac[h % 2])
                    te.matmul(pPV0[:, :], vb[:, kc, :], exb[:, 0:512],
                              start=(kc == 0), stop=(kc == 31))
                    cpvs.append(p.inc("te", te.matmul(pPV1[:, :], vb[:, kc, :],
                                                      exb[:, 512:1024],
                                                      start=(kc == 0),
                                                      stop=(kc == 31))))

                for kc in range(32):
                    pS, pS2 = qk_banks(kc)
                    if kc == 4:
                        p.wait(te, "ac", cac_rdenb)
                    elif 1 <= kc <= 3:
                        p.wait(te, "ac", cacs[kc - 1])
                    elif kc >= 5:
                        p.wait(te, "ac", cacs[kc - 2])
                    te.matmul(pS[:, :], k_fm[:, h, kc * 128:(kc + 1) * 128],
                              q_fm[:, h, 0:512], start=True, stop=True)
                    cte = p.inc("te", te.matmul(pS2[:, :],
                                                k_fm[:, h, kc * 128:(kc + 1) * 128],
                                                q_fm[:, h, 512:1024],
                                                start=True, stop=True))
                    if kc >= 1:
                        emit_pv(kc - 1)
                    p.wait(ac, "te", cte)
                    if kc >= 2:
                        p.wait(ac, "te", cpvs[kc - 2])
                        p.wait(ac, "ve", cves[kc - 2])
                    exb = exps[kc % 2]
                    ac.activation(exb[:, 0:512], pS[:, :], AF.Exp)
                    cac = p.inc("ac", ac.activation(exb[:, 512:1024], pS2[:, :], AF.Exp))
                    cacs.append(cac)
                    p.wait(ve, "ac", cac)
                    cves.append(p.inc("ve", ve.tensor_add(dsum[:], dsum[:], exb[:])))
                emit_pv(31)
                ev_pv_head[h] = cpvs[-1]
                p.wait(te, "ve", cves[-1])
                te.matmul(pA[0:1, 0:512], ones_k[:, :], dsum[:, 0:512],
                          start=True, stop=True)
                cte_den = p.inc("te", te.matmul(pC[0:1, 0:512], ones_k[:, :],
                                                dsum[:, 512:1024],
                                                start=True, stop=True))
                p.wait(ve, "te", cte_den)
                ve.tensor_copy(den_row[0:1, 0:512], pA[0:1, 0:512])
                ve.tensor_copy(den_row[0:1, 512:1024], pC[0:1, 0:512])
                ve.drain()
                ve.reciprocal(den_row[:, :], den_row[:, :])
                cve_denrow = p.inc("ve", ve.drain())
                ve.memset(dsum[:], 0.0)
                p.wait(te, "ve", cve_denrow)
                te.matmul(pA[:, :], ones_r[:, :], den_row[0:1, 0:512],
                          start=True, stop=True)
                cte_b = p.inc("te", te.matmul(pC[:, :], ones_r[:, :],
                                              den_row[0:1, 512:1024],
                                              start=True, stop=True))
                p.wait(ac, "te", cte_b)
                p.wait(ac, "ve", ev_ve_attn)
                ac.copy(rdenb[:, 0:512], pA[:, :])
                cac_rdenb = p.inc("ac", ac.copy(rdenb[:, 512:1024], pC[:, :]))
                p.wait(ve, "ac", cac_rdenb)
                ve.tensor_mul(attn_fm[:, h, 0:512], pPV0[:, :], rdenb[:, 0:512])
                ev_ve_attn = p.inc("ve", ve.tensor_mul(attn_fm[:, h, 512:1024],
                                                       pPV1[:, :], rdenb[:, 512:1024]))
                ev_evac[h % 2] = ev_ve_attn
            ev_pv_prev = ev_pv_head[HEADS - 1]
            p.wait(te, "ve", ev_ve_attn)
            p.wait(te, "ac", cac_rdenb)
            for tt in range(QT // 128):
                r0 = q0 + tt * 128
                t0 = tt * 128
                xt = x_t[tt % 2]
                tx = p.dma(sy, "ldB", xt[:], x_in[r0:r0 + 128, :])
                for half in range(2):
                    pb = pA if half == 0 else pB
                    for h in range(HEADS):
                        mi = te.matmul(pb[:, :], attn_fm[:, h, t0:t0 + 128],
                                       wo_sb[:, h, half * 512:(half + 1) * 512],
                                       start=(h == 0), stop=(h == HEADS - 1))
                cte = p.inc("te", mi)
                p.wait(ve, "te", cte)
                p.wait(ve, "ldB", tx)
                p.wait(ve, "ac", c_boutb)
                p.wait(ve, "stR")
                ve.tensor_add(s1_t[:, 0:512], pA[:, :], boutb[:, 0:512])
                ve.tensor_add(s1_t[:, 512:1024], pB[:, :], boutb[:, 512:1024])
                cve = p.inc("ve", ve.tensor_add(s1_t[:], s1_t[:], xt[:]))
                p.wait(te, "ve", cve)
                p.wait(sy, "ve", cve)
                p.dma(sy, "stR", s1_out[r0:r0 + 128, :], s1_t[:])
                ve.reduce_sum(sqs[:], s1_t[:], axis=AX, negate=True)
                ve.drain()
                ve.tensor_scalar_mul(mcol[:], sqs[:], 1.0 / D)
                ve.drain()
                cve = p.inc("ve", ve.tensor_scalar_add(tmp_t[:], s1_t[:], mcol[:, 0:1]))
                p.wait(ac, "ve", cve)
                ac.activation(hi32_t[:], tmp_t[:], AF.Square, accum_out=sqs[:, 0:1])
                ac.drain()
                ac.activation(vcol[:], sqs[:, 0:1], AF.Ln, scale=1.0 / D, bias=eps_col[:, 0:1])
                ac.drain()
                ac.activation(vcol[:], vcol[:], AF.Exp, scale=-0.5)
                cac = p.inc("ac", ac.drain())
                p.wait(ve, "ac", cac)
                ve.tensor_scalar(lnbf_t[:], tmp_t[:], vcol[:, 0:1], None, op0=ALU.mult)
                ve.tensor_scalar(tmp_t[:], tmp_t[:], vcol[:, 0:1], None, op0=ALU.mult)
                ve.drain()
                ve.tensor_copy(hi32_t[:], lnbf_t[:])
                ve.drain()
                ve.tensor_sub(tmp_t[:], tmp_t[:], hi32_t[:])
                ve.drain()
                ve.tensor_copy(lo_t[:], tmp_t[:])
                cve = p.inc("ve", ve.drain())
                p.wait(sy, "ve", cve)
                p.dma(sy, "stR", rows_out[r0:r0 + 128, :], lnbf_t[:])
                p.dma(sy, "stR", lo_rows[r0:r0 + 128, :], lo_t[:])
            t_rows_q = p.tok("stR")
            ev_te_outproj = p.tok("te")
            p.wait(sy, "stR", t_rows_q)
            p.wait(sy, "te", ev_te_outproj)
            for j in range(DCH):
                p.dma(sy, "trA", hi_fm[:, j, :],
                      rows_out[q0:q0 + QT, j * 128:(j + 1) * 128], transpose=True)
                p.dma(sy, "trA", lo_fm[:, j, :],
                      lo_rows[q0:q0 + QT, j * 128:(j + 1) * 128], transpose=True)
            t_rfm = p.tok("trA")
            p.wait(te, "trA", t_rfm)
            p.wait(te, "ve", c_wg)
            for tt in range(QT // 128):
                r0 = q0 + tt * 128
                t0 = tt * 128
                p.wait(te, "ve", ev_ve_router_prev)
                for j in range(DCH):
                    te.matmul(pR[:, 0:NEXP], hi_fm[:, j, t0:t0 + 128],
                              wg_hi[:, j, :], start=(j == 0), stop=False)
                for j in range(DCH):
                    te.matmul(pR[:, 0:NEXP], hi_fm[:, j, t0:t0 + 128],
                              wg_lo[:, j, :], start=False, stop=False)
                for j in range(DCH):
                    mi = te.matmul(pR[:, 0:NEXP], lo_fm[:, j, t0:t0 + 128],
                                   wg_hi[:, j, :], start=False, stop=(j == DCH - 1))
                    if j == DCH - 1:
                        cte = p.inc("te", mi)
                p.wait(ve, "te", cte)
                p.wait(ve, "ac", c_bgb)
                ve.tensor_add(logit_t[:, 0:NEXP], pR[:, 0:NEXP], bgb[:, :])
                cve = p.inc("ve", ve.drain())
                p.wait(te, "ve", cve)
                ve.reduce_max(mx_t[:], logit_t[:, 0:32], axis=AX)
                ve.drain()
                ve.tensor_scalar(eq_t[:], logit_t[:, 0:NEXP], mx_t[:, 0:1],
                                 None, op0=ALU.subtract)
                cve = p.inc("ve", ve.drain())
                p.wait(ac, "ve", cve)
                ac.activation(eq_t[:], eq_t[:], AF.Exp, accum_out=ssum_t[:, 0:1])
                cac = p.inc("ac", ac.drain())
                p.wait(ve, "ac", cac)
                ve.reciprocal(gate_col[:], ssum_t[:])
                ve.drain()
                ve.tensor_scalar(eq_t[:], logit_t[:, 0:NEXP], mx_t[:, 0:1], None, op0=ALU.is_ge)
                ve.drain()
                ve.tensor_mul(cand_t[:, 0:NEXP], eq_t[:], iota_me[:])
                ve.drain()
                ve.tensor_scalar_add(cand_t[:, 0:NEXP], cand_t[:, 0:NEXP], 65536.0)
                ve.drain()
                ve.tensor_reduce(idx_col[:], cand_t[:, 0:32], axis=AX, op=ALU.min)
                ve.drain()
                ve.tensor_scalar(ohf_t[:], iota_e[:], idx_col[:, 0:1], None,
                                 op0=ALU.is_equal)
                ve.drain()
                ve.tensor_copy(oh_t[:], ohf_t[:])
                cve = p.inc("ve", ve.drain())
                p.wait(te, "ve", cve)
                te.matmul(pR[:, 0:NEXP], trilbf[:, :], oh_t[:, :], start=True, stop=True)
                te.matmul(pE[:, 0:NEXP], ones_r[:, :], run_tot[:, :],
                          start=True, stop=True)
                cte2 = p.inc("te", te.matmul(pG[0:1, 0:NEXP], ones_k[:, :], ohf_t[:, :],
                                             start=True, stop=True))
                p.wait(ac, "te", cte2)
                ac.copy(runb[:, :], pE[:, 0:NEXP])
                cac = p.inc("ac", ac.drain())
                p.wait(ve, "te", cte2)
                p.wait(ve, "ac", cac)
                ve.tensor_add(gsel_t[:, 0:NEXP], pR[:, 0:NEXP], runb[:, :])
                ve.drain()
                ve.tensor_mul(gsel_t[:, 0:NEXP], gsel_t[:, 0:NEXP], ohf_t[:])
                ve.drain()
                ve.tensor_reduce(lpos_col[:], gsel_t[:, 0:32], axis=AX, op=ALU.add)
                ve.drain()
                ve.tensor_add(run_tot[:, :], run_tot[:, :], pG[0:1, 0:NEXP])
                ve.drain()
                ve.tensor_copy(idx_acc[:, tile_i:tile_i + 1], idx_col[:])
                ve.tensor_copy(gate_acc[:, tile_i:tile_i + 1], gate_col[:])
                ve.tensor_copy(lpos_acc[:, tile_i:tile_i + 1], lpos_col[:])
                cve = p.inc("ve", ve.drain())
                ev_ve_router_prev = cve
                p.wait(te, "ve", cve)
                ev_te_qtile = p.tok("te")
                tile_i += 1
        cve = p.tok("ve")
        p.wait(sy, "ve", cve)
        p.dma(sy, "stR", idx_out[:, :], idx_acc[:, :])
        p.dma(sy, "stR", gate_out[:, :], gate_acc[:, :])
        p.dma(sy, "stR", lpos_out[:, :], lpos_acc[:, :])
        p.dma(sy, "stR", cnt_out[:, :], run_tot[:, :])
        p.wait(sy, "stR")
    return nc



_CACHE = {}
_DBG = {}


def _get(name, builder):
    if name not in _CACHE:
        _CACHE[name] = builder()
    return _CACHE[name]


def kernel(**inputs):
    raise NotImplementedError("wired up after L2..L5 exist")


TOKALL = TOK * NCORE
NTALL = NT * NCORE


def build_L2():
    nc = bass.Bass("TRN2", target_bir_lowering=False, debug=False,
                   detect_race_conditions=False)
    st = ExitStack()
    rows_in = nc.dram_tensor("rows_all", [TOKALL, D], BF16, kind="ExternalInput").ap()
    idx_in = nc.dram_tensor("idx_all", [128, NTALL], F32, kind="ExternalInput").ap()
    lpos_in = nc.dram_tensor("lpos_all", [128, NTALL], F32, kind="ExternalInput").ap()
    cnt_in = nc.dram_tensor("cnt_all", [8, NEXP], F32, kind="ExternalInput").ap()
    w1_in = nc.dram_tensor("w1", [EXP_PC, D, HID], F32, kind="ExternalInput").ap()
    w2_in = nc.dram_tensor("w2", [EXP_PC, HID, D], F32, kind="ExternalInput").ap()
    b1_in = nc.dram_tensor("b1", [EXP_PC, 128, HID // 128], F32, kind="ExternalInput").ap()
    b2_in = nc.dram_tensor("b2", [EXP_PC, 1, D], F32, kind="ExternalInput").ap()
    l8_in = nc.dram_tensor("l8", [8, 8], F32, kind="ExternalInput").ap()
    mybase_in = nc.dram_tensor("mybase", [128, 1], F32, kind="ExternalInput").ap()

    y_out = nc.dram_tensor("y_slice", [SLOTS, D], BF16, kind="ExternalOutput").ap()
    dest_out = nc.dram_tensor("dest_all", [128, NTALL], F32, kind="ExternalOutput").ap()

    buf_dram = nc.dram_tensor("buf_dram", [SLOTS, D], BF16, kind="Internal").ap()

    te, ve, ac, gp, sy = nc.tensor, nc.vector, nc.scalar, nc.gpsimd, nc.sync
    with st:
        p = P(nc, st, ("gp", "te", "ve", "ac", "ldA", "ldB", "ldG", "ldH",
                       "ldI", "ldJ", "stR", "trA", "scat"))
        sb = lambda name, shape, dt: st.enter_context(nc.sbuf_tensor(name, shape, dt))
        ps = lambda name, shape: st.enter_context(nc.psum_tensor(name, shape, F32))

        dest_acc = sb("dest_acc", [128, NTALL], F32)
        destu = sb("destu", [128, NTALL], U32)
        cnt_sb = sb("cnt_sb", [8, NEXP], F32)
        l8_sb = sb("l8_sb", [8, 8], F32)
        bases_sb = sb("bases_sb", [8, NEXP], F32)
        mybase = sb("mybase_sb", [128, 1], F32)
        ones_r = sb("ones_r", [1, 128], F32)
        w1_sb = sb("w1_sb", [128, DCH, HID], BF16)
        w2_sb = sb("w2_sb", [128, HID // 128, D], BF16)
        wstg = [sb("wstg%d" % i, [128, HID // 4], F32) for i in range(2)]
        b1col = sb("b1col", [128, EXP_PC * (HID // 128)], F32)
        b2b = sb("b2b", [128, D], F32)
        b2row = sb("b2row", [1, EXP_PC * D], F32)
        buf_fm = sb("buf_fm", [128, DCH, 2048], BF16)
        h_bf = [sb("h_bf%d" % i, [128, 256], BF16) for i in range(2)]
        y_t = sb("y_t", [128, D], BF16)

        pH = [ps("pH%d" % i, [128, 512]) for i in range(2)]
        pY = [ps("pY%d" % i, [128, 512]) for i in range(4)]
        pR = ps("pR2", [128, 512])

        NHCH = HID // 128

        p.dma(sy, "ldA", cnt_sb[:], cnt_in)
        p.dma(sy, "ldA", l8_sb[:], l8_in)
        p.dma(sy, "ldA", mybase[:], mybase_in)
        for e in range(EXP_PC):
            p.dma(sy, "ldA", b1col[:, e * NHCH:(e + 1) * NHCH], b1_in[e, :, :])
            p.dma(sy, "ldA", b2row[0:1, e * D:(e + 1) * D], b2_in[e, :, :])
        t_small = p.tok("ldA")
        p.wait(ve, "ldA", t_small)
        ve.memset(ones_r[:], 1.0)
        ve.memset(y_t[:], 0.0)
        cve0 = p.inc("ve", ve.drain())

        p.wait(sy, "ve", cve0)
        for i in range(SLOTS // 128):
            p.dma(sy, "stR", buf_dram[i * 128:(i + 1) * 128, :], y_t[:])
        t_zero = p.tok("stR")

        p.wait(te, "ldA", t_small)
        p.wait(te, "ve", cve0)
        cte = p.inc("te", te.matmul(pR[0:8, 0:NEXP], l8_sb[:, :], cnt_sb[:, :],
                                    start=True, stop=True))
        p.wait(ac, "te", cte)
        ac.copy(bases_sb[:, :], pR[0:8, 0:NEXP])
        cac_b = p.inc("ac", ac.drain())

        def stage_weights(e, gate_te):
            chunks = []
            for dj in range(DCH):
                for qu in range(4):
                    chunks.append((w1_in[e, dj * 128:(dj + 1) * 128,
                                         qu * 1024:(qu + 1) * 1024],
                                   w1_sb[:, dj, qu * 1024:(qu + 1) * 1024],
                                   1024))
            for hj in range(NHCH):
                chunks.append((w2_in[e, hj * 128:(hj + 1) * 128, :],
                               w2_sb[:, hj, :], D))
            n = len(chunks)
            qs = ["ldA", "ldB"]
            ld_tok = {}

            def emit_load(i):
                src, dst, w = chunks[i]
                ld_tok[i] = p.dma(ac, qs[i % 2], wstg[i % 2][:, 0:w], src)

            last = None
            emit_load(0)
            for i in range(n):
                if i + 1 < n:
                    emit_load(i + 1)
                src, dst, w = chunks[i]
                if i == 0 and gate_te is not None:
                    p.wait(ac, "te", gate_te)
                p.wait(ac, qs[i % 2], ld_tok[i])
                ac.copy(dst, wstg[i % 2][:, 0:w])
                last = p.inc("ac", ac.drain())
            return last

        tr2 = ExitStack()
        sbt = lambda name, shape, dt: tr2.enter_context(nc.sbuf_tensor(name, shape, dt))
        idx_all = sbt("idx_all_sb", [128, NTALL], F32)
        lpos_all = sbt("lpos_all_sb", [128, NTALL], F32)
        bsel = sbt("bsel", [128, NTALL], F32)
        scrM = sbt("scrM", [128, NEXP * NT], F32)
        basb_all = sbt("basb_all", [128, NCORE * NEXP], F32)
        braw = sbt("braw", [1, NCORE * NEXP], F32)
        scr1 = scrM[:, 0:NTALL]
        scr2 = scrM[:, NTALL:2 * NTALL]

        p.dma(sy, "ldG", idx_all[:], idx_in)
        p.dma(sy, "ldG", lpos_all[:], lpos_in)
        p.wait(sy, "ac", cac_b)
        for r in range(NCORE):
            p.dma(sy, "ldG", braw[0:1, r * NEXP:(r + 1) * NEXP],
                  bases_sb[r:r + 1, 0:NEXP])
        tb = p.tok("ldG")
        p.wait(te, "ldG", tb)
        cte = p.inc("te", te.matmul(pR[:, 0:NCORE * NEXP], ones_r[:, :], braw[:, :],
                                    start=True, stop=True))
        p.wait(ac, "te", cte)
        ac.copy(basb_all[:, :], pR[:, 0:NCORE * NEXP])
        cac_bb = p.inc("ac", ac.drain())
        p.wait(ve, "ac", cac_bb)
        p.wait(ve, "ldG", tb)
        p.wait(te, "ac", cac_bb)
        c_wready = stage_weights(0, None)
        for r in range(NCORE):
            blk = slice(r * NT, r * NT + NT)
            for e in range(NEXP):
                ve.tensor_scalar(scrM[:, e * NT:(e + 1) * NT], idx_all[:, blk],
                                 float(e), None, op0=ALU.is_equal)
            ve.drain()
            for e in range(NEXP):
                ve.tensor_scalar(scrM[:, e * NT:(e + 1) * NT],
                                 scrM[:, e * NT:(e + 1) * NT],
                                 basb_all[:, r * NEXP + e:r * NEXP + e + 1],
                                 None, op0=ALU.mult)
            ve.drain()
            for step in (8, 4, 2, 1):
                for i in range(step):
                    ve.tensor_add(scrM[:, i * NT:(i + 1) * NT],
                                  scrM[:, i * NT:(i + 1) * NT],
                                  scrM[:, (i + step) * NT:(i + step + 1) * NT])
                ve.drain()
            ve.tensor_copy(bsel[:, blk], scrM[:, 0:NT])
        ve.drain()
        ve.tensor_add(bsel[:], bsel[:], lpos_all[:])
        ve.drain()
        ve.tensor_scalar(scr1[:], bsel[:], float(CAP), None, op0=ALU.is_lt)
        ve.drain()
        ve.scalar_tensor_tensor(scr2[:], idx_all[:], float(CAP), bsel[:],
                                op0=ALU.mult, op1=ALU.add)
        ve.drain()
        ve.tensor_mul(scr2[:], scr2[:], scr1[:])
        ve.drain()
        ve.tensor_scalar(scr1[:], scr1[:], -BIGF, BIGF, op0=ALU.mult, op1=ALU.add)
        ve.drain()
        ve.tensor_add(dest_acc[:], scr2[:], scr1[:])
        ve.drain()
        ve.tensor_scalar(scr2[:], dest_acc[:], mybase[:, 0:1], None, op0=ALU.subtract)
        ve.drain()
        ve.tensor_scalar(scr1[:], scr2[:], 0.0, None, op0=ALU.is_ge)
        ve.drain()
        ve.tensor_mul(scr2[:], scr2[:], scr1[:])
        ve.drain()
        ve.tensor_scalar(scr1[:], scr1[:], -BIGF, BIGF, op0=ALU.mult, op1=ALU.add)
        ve.drain()
        ve.tensor_add(scr2[:], scr2[:], scr1[:])
        ve.drain()
        ve.tensor_copy(destu[:], scr2[:])
        cve_dest = p.inc("ve", ve.drain())
        p.wait(sy, "ve", cve_dest)
        p.dma(sy, "stR", dest_out[:, :], dest_acc[:, :])
        tr2.close()
        row_t = [sb("row_t%d" % i, [128, D], BF16) for i in range(4)]

        p.wait(gp, "ve", cve_dest)
        p.wait(gp, "stR", t_zero)
        bc_reg = gp.to_reg(SLOTS - 1)
        ev_sc = {}
        ldq = ["ldG", "ldH", "ldI", "ldJ"]
        for g in range(NTALL):
            par = g % 4
            if g >= 4:
                p.wait(sy, "scat", ev_sc[g - 4])
            tl = p.dma(sy, ldq[par], row_t[par][:], rows_in[g * 128:(g + 1) * 128, :])
            p.wait(gp, ldq[par], tl)
            ev_sc[g] = p.inc("scat", gp.indirect_dma_start(
                out=buf_dram,
                out_offset=IndirectOffsetOnAxis(destu[:, g:g + 1], axis=0),
                in_=row_t[par][:], in_offset=None,
                bounds_check=bc_reg, oob_is_err=False), 16)
        t_scat = p.tok("scat")

        for e in range(EXP_PC):
            if e > 0:
                c_wready = stage_weights(e, p.tok("te"))
            p.wait(te, "ldA", t_small)
            p.wait(te, "ve", p.tok("ve"))
            cte = p.inc("te", te.matmul(pR[:, 0:512], ones_r[:, :],
                                        b2row[0:1, e * D:e * D + 512],
                                        start=True, stop=True))
            p.wait(ac, "te", cte)
            ac.copy(b2b[:, 0:512], pR[:, 0:512])
            cacb = p.inc("ac", ac.drain())
            p.wait(te, "ac", cacb)
            cte = p.inc("te", te.matmul(pR[:, 0:512], ones_r[:, :],
                                        b2row[0:1, e * D + 512:(e + 1) * D],
                                        start=True, stop=True))
            p.wait(ac, "te", cte)
            ac.copy(b2b[:, 512:1024], pR[:, 0:512])
            cacb2 = p.inc("ac", ac.drain())
            c_wall = max(c_wready, cacb2)

            for stile in range(CAP // 2048):
                s0 = e * CAP + stile * 2048
                p.wait(sy, "scat", t_scat)
                p.wait(sy, "te", p.tok("te"))
                for dj in range(DCH):
                    p.dma(sy, "trA", buf_fm[:, dj, :],
                          buf_dram[s0:s0 + 2048, dj * 128:(dj + 1) * 128],
                          transpose=True)
                t_fm = p.tok("trA")
                p.wait(te, "trA", t_fm)
                p.wait(te, "ac", c_wall)
                for blk in range(8):
                    b0 = blk * 256
                    evh = {}
                    for hj in range(NHCH):
                        par = hj % 2
                        if hj >= 2:
                            p.wait(te, "ac", evh[hj - 2])
                        for dj in range(DCH):
                            mi = te.matmul(pH[par][:, 0:256],
                                           w1_sb[:, dj, hj * 128:(hj + 1) * 128],
                                           buf_fm[:, dj, b0:b0 + 256],
                                           start=(dj == 0), stop=(dj == DCH - 1))
                        cte = p.inc("te", mi)
                        p.wait(ac, "te", cte)
                        ac.activation(h_bf[par][:, :], pH[par][:, 0:256], AF.Relu,
                                      bias=b1col[:, e * NHCH + hj:e * NHCH + hj + 1])
                        cach = p.inc("ac", ac.drain())
                        evh[hj] = cach
                        p.wait(te, "ac", cach)
                        for s128 in range(2):
                            for half in range(2):
                                mi = te.matmul(
                                    pY[s128 * 2 + half][:, :],
                                    h_bf[par][:, s128 * 128:(s128 + 1) * 128],
                                    w2_sb[:, hj, half * 512:(half + 1) * 512],
                                    start=(hj == 0), stop=(hj == NHCH - 1))
                        if hj == NHCH - 1:
                            cte_y = p.inc("te", mi)
                    p.wait(ve, "te", cte_y)
                    p.wait(ve, "ac", cacb2)
                    for s128 in range(2):
                        r0 = s0 + b0 + s128 * 128
                        p.wait(ve, "stR", p.tok("stR"))
                        ve.tensor_add(y_t[:, 0:512], pY[s128 * 2][:, :], b2b[:, 0:512])
                        ve.tensor_add(y_t[:, 512:1024], pY[s128 * 2 + 1][:, :],
                                      b2b[:, 512:1024])
                        cve = p.inc("ve", ve.drain())
                        p.wait(sy, "ve", cve)
                        p.dma(sy, "stR", y_out[r0:r0 + 128, :], y_t[:])
                    p.wait(te, "ve", p.tok("ve"))
        p.wait(sy, "stR")
    return nc


def _emit_gather_add(nc, p, sb_alloc, y_all_ap, dest_ap, gate_ap, s_in_ap, s_out_dram,
                     bc_tokall):
    te, ve, ac, gp, sy = nc.tensor, nc.vector, nc.scalar, nc.gpsimd, nc.sync
    dest_sb = sb_alloc("ga_dest", [128, NT], F32)
    gate_sb = sb_alloc("ga_gate", [128, NT], F32)
    destu = sb_alloc("ga_destu", [128, NT], U32)
    yrow = [sb_alloc("ga_yrow%d" % i, [128, D], BF16) for i in range(2)]
    s_t = sb_alloc("ga_s", [128, D], F32)
    o_t = sb_alloc("ga_o", [128, D], F32)
    t1 = p.dma(sy, "ldA", dest_sb[:], dest_ap)
    t2 = p.dma(sy, "ldA", gate_sb[:], gate_ap)
    p.wait(ve, "ldA", max(t1, t2))
    ve.tensor_copy(destu[:], dest_sb[:])
    ve.memset(yrow[0][:], 0.0)
    ve.memset(yrow[1][:], 0.0)
    cve0 = p.inc("ve", ve.drain())
    p.wait(gp, "ve", cve0)
    ev_use = {}
    for t in range(NT):
        par = t % 2
        if t >= 2:
            p.wait(gp, "ve", ev_use[t - 2])
        p.inc("scat", gp.indirect_dma_start(
            out=yrow[par][:], out_offset=None,
            in_=y_all_ap, in_offset=IndirectOffsetOnAxis(destu[:, t:t + 1], axis=0),
            bounds_check=bc_tokall, oob_is_err=False), 16)
        ts = p.dma(sy, "ldB", s_t[:], s_in_ap[t * 128:(t + 1) * 128, :])
        p.wait(ve, "scat", p.tok("scat"))
        p.wait(ve, "ldB", ts)
        p.wait(ve, "stR", p.tok("stR"))
        ve.scalar_tensor_tensor(o_t[:], yrow[par][:], gate_sb[:, t:t + 1], s_t[:],
                                op0=ALU.mult, op1=ALU.add)
        cve = p.inc("ve", ve.drain())
        ve.memset(yrow[par][:], 0.0)
        ev_use[t] = p.inc("ve", ve.drain())
        p.wait(sy, "ve", cve)
        p.dma(sy, "stR", s_out_dram[t * 128:(t + 1) * 128, :], o_t[:])
    return p.tok("stR")


def build_L3():
    nc = bass.Bass("TRN2", target_bir_lowering=False, debug=False,
                   detect_race_conditions=False)
    st = ExitStack()
    s1_in = nc.dram_tensor("s1", [TOK, D], F32, kind="ExternalInput").ap()
    yall_in = nc.dram_tensor("y_all", [TOKALL, D], BF16, kind="ExternalInput").ap()
    destm_in = nc.dram_tensor("dest_mine", [128, NT], F32, kind="ExternalInput").ap()
    gatem_in = nc.dram_tensor("gate_mine", [128, NT], F32, kind="ExternalInput").ap()
    inw_in = nc.dram_tensor("in_w", [D, 3 * CIN], F32, kind="ExternalInput").ap()
    inb_in = nc.dram_tensor("in_b", [128, 3 * CIN // 128], F32, kind="ExternalInput").ap()
    convw_in = nc.dram_tensor("conv_w", [128, CIN // 128, 3], F32, kind="ExternalInput").ap()
    convb_in = nc.dram_tensor("conv_b", [128, CIN // 128], F32, kind="ExternalInput").ap()
    alog_in = nc.dram_tensor("a_log", [128, CIN // 128 * DST], F32, kind="ExternalInput").ap()
    bp_in = nc.dram_tensor("b_p", [128, CIN // 128 * DST], F32, kind="ExternalInput").ap()
    cp_in = nc.dram_tensor("c_p", [128, CIN // 128 * DST], F32, kind="ExternalInput").ap()
    dp_in = nc.dram_tensor("d_p", [128, CIN // 128], F32, kind="ExternalInput").ap()
    outw_in = nc.dram_tensor("out_w", [CIN, D], F32, kind="ExternalInput").ap()
    outb_in = nc.dram_tensor("out_b", [1, D], F32, kind="ExternalInput").ap()
    gam_in = nc.dram_tensor("gam", [1, D], F32, kind="ExternalInput").ap()
    bet_in = nc.dram_tensor("bet", [1, D], F32, kind="ExternalInput").ap()
    wg_in = nc.dram_tensor("wg", [D, NEXP], F32, kind="ExternalInput").ap()
    bg_in = nc.dram_tensor("bg", [1, NEXP], F32, kind="ExternalInput").ap()

    s3_out = nc.dram_tensor("s3", [TOK, D], F32, kind="ExternalOutput").ap()
    rows_out = nc.dram_tensor("rows2", [TOK, D], BF16, kind="ExternalOutput").ap()
    idx_out = nc.dram_tensor("idx2", [128, NT], F32, kind="ExternalOutput").ap()
    gate_out = nc.dram_tensor("gate2", [128, NT], F32, kind="ExternalOutput").ap()
    lpos_out = nc.dram_tensor("lpos2", [128, NT], F32, kind="ExternalOutput").ap()
    cnt_out = nc.dram_tensor("cnt2", [1, NEXP], F32, kind="ExternalOutput").ap()

    s2_dram = nc.dram_tensor("s2_dram", [TOK, D], F32, kind="Internal").ap()
    h_rows = nc.dram_tensor("h_rows", [TOK, D], BF16, kind="Internal").ap()
    uvg_fm = nc.dram_tensor("uvg_fm", [3 * CIN, TOK], BF16, kind="Internal").ap()
    yg_fm = nc.dram_tensor("yg_fm", [CIN, TOK], BF16, kind="Internal").ap()
    lo_rows = nc.dram_tensor("lo_rows2", [TOK, D], BF16, kind="Internal").ap()

    te, ve, ac, gp, sy = nc.tensor, nc.vector, nc.scalar, nc.gpsimd, nc.sync
    NG = CIN // 128
    NIN = 3 * CIN // 128

    def barrier(p):
        for eng in (ve, ac, te, gp, sy):
            for key in p.sem:
                p.wait(eng, key)

    with st:
        p = P(nc, st, ("gp", "te", "ve", "ac", "ldA", "ldB", "ldG", "ldH", "stR",
                       "stS", "trA", "scat"))
        sb = lambda name, shape, dt: st.enter_context(nc.sbuf_tensor(name, shape, dt))
        ps = lambda name, shape: st.enter_context(nc.psum_tensor(name, shape, F32))

        gamb = sb("gamb", [128, D], F32)
        betb = sb("betb", [128, D], F32)
        outbb = sb("outbb", [128, D], F32)
        wg_hi = sb("wg_hi3", [128, DCH, NEXP], BF16)
        wg_lo = sb("wg_lo3", [128, DCH, NEXP], BF16)
        bgb = sb("bgb3", [128, NEXP], F32)
        ones_r = sb("ones_r3", [1, 128], F32)
        ones_k = sb("ones_k3", [128, 1], F32)
        trilbf = sb("trilbf3", [128, 128], BF16)
        eye_bf = sb("eye_bf", [128, 128], BF16)
        iota_e = sb("iota_e3", [128, NEXP], F32)
        iota_me = sb("iota_me3", [128, NEXP], F32)
        run_tot = sb("run_tot3", [1, NEXP], F32)
        inb_sb = sb("inb_sb", [128, NIN], F32)
        convw_sb = sb("convw_sb", [128, NG, 3], F32)
        convb_sb = sb("convb_sb", [128, NG], F32)
        dp_sb = sb("dp_sb", [128, NG], F32)
        ktaps = sb("ktaps", [128, TAPS, NG], F32)
        mcol = sb("mcol3", [128, 1], F32)
        eps_col = sb("eps_col3", [128, 1], F32)
        sqs = sb("sqs3", [128, 1], F32)
        vcol = sb("vcol3", [128, 1], F32)
        idx_acc = sb("idx_acc3", [128, NT], F32)
        gate_acc = sb("gate_acc3", [128, NT], F32)
        lpos_acc = sb("lpos_acc3", [128, NT], F32)
        logit_t = sb("logit_t3", [128, 32], F32)
        mx_t = sb("mx_t3", [128, 1], F32)
        ssum_t = sb("ssum_t3", [128, 1], F32)
        gate_col = sb("gate_col3", [128, 1], F32)
        eq_t = sb("eq_t3", [128, NEXP], F32)
        cand_t = sb("cand_t3", [128, 32], F32)
        idx_col = sb("idx_col3", [128, 1], F32)
        oh_t = sb("oh_t3", [128, NEXP], BF16)
        ohf_t = sb("ohf_t3", [128, NEXP], F32)
        gsel_t = sb("gsel_t3", [128, 32], F32)
        lpos_col = sb("lpos_col3", [128, 1], F32)
        runb = sb("runb3", [128, NEXP], F32)
        s_t = sb("s_t3", [128, D], F32)
        tmp_t = sb("tmp_t3", [128, D], F32)
        tmp2_t = sb("tmp2_t3", [128, D], F32)
        hi32_t = sb("hi32_t3", [128, D], F32)
        lnbf_t = sb("lnbf_t3", [128, D], BF16)
        lo_t = sb("lo_t3", [128, D], BF16)

        pA = ps("pA3", [128, 512])
        pB = ps("pB3", [128, 512])
        pC = ps("pC3", [128, 512])
        pD = ps("pD3", [128, 512])
        pR = ps("pR3", [128, 512])
        pG3 = ps("pG3", [128, 512])

        bc_reg = gp.to_reg(TOKALL - 1)
        ga = ExitStack()
        sba = lambda name, shape, dt: ga.enter_context(nc.sbuf_tensor(name, shape, dt))
        _emit_gather_add(nc, p, sba, yall_in, destm_in, gatem_in, s1_in,
                         s2_dram, bc_reg)
        barrier(p)
        ga.close()

        gp.iota(iota_e[:], pattern=[[1, NEXP]], base=0, channel_multiplier=0,
                allow_small_or_imprecise_dtypes=True)
        gp.iota(tmp_t[:, 0:128], pattern=[[1, 128]], base=0, channel_multiplier=0,
                allow_small_or_imprecise_dtypes=True)
        c_iota = p.inc("gp", gp.iota(mcol[:], pattern=[[0, 1]], base=0,
                                     channel_multiplier=1,
                                     allow_small_or_imprecise_dtypes=True))
        p.wait(ve, "gp", c_iota)
        ve.memset(ones_r[:], 1.0)
        ve.memset(ones_k[:], 1.0)
        ve.memset(run_tot[:], 0.0)
        ve.memset(eps_col[:], EPS)
        ve.memset(logit_t[:, 16:32], -1.0e30)
        ve.memset(cand_t[:, 16:32], 1.0e30)
        ve.memset(gsel_t[:, 16:32], 0.0)
        ve.tensor_scalar_add(iota_me[:], iota_e[:], -65536.0)
        ve.drain()
        ve.tensor_scalar(tmp2_t[:, 0:128], tmp_t[:, 0:128], mcol[:, 0:1], None,
                         op0=ALU.is_gt)
        ve.drain()
        ve.tensor_copy(trilbf[:], tmp2_t[:, 0:128])
        ve.drain()
        ve.tensor_scalar(tmp2_t[:, 0:128], tmp_t[:, 0:128], mcol[:, 0:1], None,
                         op0=ALU.is_equal)
        ve.drain()
        ve.tensor_copy(eye_bf[:], tmp2_t[:, 0:128])
        c_const = p.inc("ve", ve.drain())

        p.dma(sy, "ldA", inb_sb[:], inb_in)
        p.dma(sy, "ldA", convw_sb[:], convw_in)
        p.dma(sy, "ldA", convb_sb[:], convb_in)
        p.dma(sy, "ldA", dp_sb[:], dp_in)
        t_par = p.tok("ldA")
        grow = sb("grow", [1, D], F32)
        for src_ap, dstb in ((gam_in, gamb), (bet_in, betb), (outb_in, outbb)):
            tg = p.dma(sy, "ldB", grow[:], src_ap)
            p.wait(te, "ldB", tg)
            p.wait(te, "ve", c_const)
            for half in range(2):
                cte = p.inc("te", te.matmul(pR[:, :], ones_r[:, :],
                                            grow[:, half * 512:(half + 1) * 512],
                                            start=True, stop=True))
                p.wait(ac, "te", cte)
                ac.copy(dstb[:, half * 512:(half + 1) * 512], pR[:, :])
                cac = p.inc("ac", ac.drain())
                p.wait(te, "ac", cac)
            p.wait(sy, "ac", p.tok("ac"))
        bg_row = sb("bg_row3", [1, NEXP], F32)
        tg = p.dma(sy, "ldB", bg_row[:], bg_in)
        p.wait(te, "ldB", tg)
        cte = p.inc("te", te.matmul(pR[:, 0:NEXP], ones_r[:, :], bg_row[:, :],
                                    start=True, stop=True))
        p.wait(ac, "te", cte)
        ac.copy(bgb[:, :], pR[:, 0:NEXP])
        c_bgb = p.inc("ac", ac.drain())
        p.wait(te, "ac", c_bgb)
        wg32 = sb("wg32_3", [128, DCH, NEXP], F32)
        wg_t32 = sb("wg_t32_3", [128, DCH, NEXP], F32)
        for j in range(DCH):
            p.dma(sy, "ldB", wg32[:, j, :], wg_in[j * 128:(j + 1) * 128, :])
        t_wg = p.tok("ldB")
        p.wait(ve, "ldB", t_wg)
        ve.tensor_copy(wg_hi[:, :, :], wg32[:, :, :])
        ve.drain()
        ve.tensor_copy(wg_t32[:, :, :], wg_hi[:, :, :])
        ve.drain()
        ve.tensor_sub(wg_t32[:, :, :], wg32[:, :, :], wg_t32[:, :, :])
        ve.drain()
        c_wg = p.inc("ve", ve.tensor_copy(wg_lo[:, :, :], wg_t32[:, :, :]))

        prow = sb("prow", [128, NG * DST], F32)
        rdec = sb("rdec", [128, NG * DST], F32)
        tB = p.dma(sy, "ldB", prow[:], bp_in)
        tAg = p.dma(sy, "ldB", rdec[:], alog_in)
        p.wait(ac, "ldB", tAg)
        ac.activation(rdec[:], rdec[:], AF.Exp)
        ac.drain()
        ac.activation(rdec[:], rdec[:], AF.Exp, scale=-1.0)
        cac = p.inc("ac", ac.drain())
        tC = p.dma(sy, "ldB", tmp_t[:, 0:NG * DST], cp_in)
        p.wait(ve, "ldB", tC)
        p.wait(ve, "ac", cac)
        ve.tensor_mul(prow[:], prow[:], tmp_t[:, 0:NG * DST])
        ve.drain()
        pv = prow.rearrange("p (g j) -> p g j", g=NG)
        for dl in range(TAPS):
            ve.tensor_reduce(ktaps[:, dl, :], pv, axis=AX, op=ALU.add)
            ve.drain()
            if dl + 1 < TAPS:
                ve.tensor_mul(prow[:], prow[:], rdec[:])
                ve.drain()
        p.wait(ve, "ldA", t_par)
        ve.tensor_add(ktaps[:, 0, :], ktaps[:, 0, :], dp_sb[:, :])
        c_taps = p.inc("ve", ve.drain())

        phA = ExitStack()
        sbA = lambda name, shape, dt: phA.enter_context(nc.sbuf_tensor(name, shape, dt))
        h_fm = sbA("h_fm3", [128, DCH, 2048], BF16)
        inw_sb = sbA("inw_sb", [128, DCH, 3 * CIN // 2], BF16)
        uvg_t = [sbA("uvg_t%d" % i, [128, 512], BF16) for i in range(4)]
        QT = 2048
        ev_te_q = p.tok("te")
        ev_ve_prev = p.tok("ve")
        ev_ac_u = {}
        ev_st_u = {}
        nU = 0
        for qi in range(TOK // QT):
            q0 = qi * QT
            p.wait(sy, "ve", ev_ve_prev)
            for tt in range(QT // 128):
                r0 = q0 + tt * 128
                ts = p.dma(sy, "ldA", s_t[:], s2_dram[r0:r0 + 128, :])
                p.wait(ve, "ldA", ts)
                p.wait(ve, "stR", p.tok("stR"))
                ve.reduce_sum(sqs[:], s_t[:], axis=AX, negate=True)
                ve.drain()
                ve.tensor_scalar_mul(mcol[:], sqs[:], 1.0 / D)
                ve.drain()
                cve = p.inc("ve", ve.tensor_scalar_add(tmp_t[:], s_t[:], mcol[:, 0:1]))
                p.wait(ac, "ve", cve)
                ac.activation(hi32_t[:], tmp_t[:], AF.Square, accum_out=sqs[:, 0:1])
                ac.drain()
                ac.activation(vcol[:], sqs[:, 0:1], AF.Ln, scale=1.0 / D,
                              bias=eps_col[:, 0:1])
                ac.drain()
                ac.activation(vcol[:], vcol[:], AF.Exp, scale=-0.5)
                cac = p.inc("ac", ac.drain())
                p.wait(ve, "ac", cac)
                ve.tensor_scalar(tmp2_t[:], tmp_t[:], vcol[:, 0:1], None, op0=ALU.mult)
                ve.drain()
                ve.reduce_sum(sqs[:], tmp2_t[:], axis=AX, negate=True)
                ve.drain()
                ve.tensor_scalar_mul(mcol[:], sqs[:], 1.0 / D)
                ve.drain()
                cve = p.inc("ve", ve.tensor_scalar_add(tmp_t[:], tmp2_t[:], mcol[:, 0:1]))
                p.wait(ac, "ve", cve)
                ac.activation(hi32_t[:], tmp_t[:], AF.Square, accum_out=sqs[:, 0:1])
                ac.drain()
                ac.activation(vcol[:], sqs[:, 0:1], AF.Ln, scale=1.0 / D,
                              bias=eps_col[:, 0:1])
                ac.drain()
                ac.activation(vcol[:], vcol[:], AF.Exp, scale=-0.5)
                cac = p.inc("ac", ac.drain())
                p.wait(ve, "ac", cac)
                ve.tensor_scalar(tmp_t[:], tmp_t[:], vcol[:, 0:1], None, op0=ALU.mult)
                ve.drain()
                ve.tensor_mul(tmp_t[:], tmp_t[:], gamb[:])
                ve.drain()
                ve.tensor_add(tmp_t[:], tmp_t[:], betb[:])
                ve.drain()
                ve.tensor_copy(lnbf_t[:], tmp_t[:])
                cve = p.inc("ve", ve.drain())
                p.wait(sy, "ve", cve)
                p.dma(sy, "stR", h_rows[r0:r0 + 128, :], lnbf_t[:])
            t_hq = p.tok("stR")
            p.wait(sy, "stR", t_hq)
            p.wait(sy, "te", ev_te_q)
            for j in range(DCH):
                p.dma(sy, "trA", h_fm[:, j, :],
                      h_rows[q0:q0 + QT, j * 128:(j + 1) * 128], transpose=True)
            t_fm = p.tok("trA")
            p.wait(te, "trA", t_fm)
            for half2 in range(2):
                ch0 = half2 * (NIN // 2)
                p.wait(gp, "te", p.tok("te"))
                for j in range(DCH):
                    p.dma(gp, "ldG", inw_sb[:, j, :],
                          inw_in[j * 128:(j + 1) * 128,
                                 ch0 * 128:(ch0 + NIN // 2) * 128])
                t_inw = p.tok("ldG")
                p.wait(te, "ldG", t_inw)
                for cc in range(NIN // 2):
                    cch = ch0 + cc
                    for nt4 in range(QT // 512):
                        pb = pA if (nU % 2 == 0) else pB
                        if nU >= 2:
                            p.wait(te, "ac", ev_ac_u[nU - 2])
                        for j in range(DCH):
                            mi = te.matmul(pb[:, :],
                                           inw_sb[:, j, cc * 128:(cc + 1) * 128],
                                           h_fm[:, j, nt4 * 512:(nt4 + 1) * 512],
                                           start=(j == 0), stop=(j == DCH - 1))
                        cte = p.inc("te", mi)
                        ub = uvg_t[nU % 4]
                        stq = "stR" if nU % 2 == 0 else "stS"
                        p.wait(ac, "te", cte)
                        if nU >= 4:
                            p.wait(ac, ev_st_u[nU - 4][0], ev_st_u[nU - 4][1])
                        ac.activation(ub[:], pb[:, :], AF.Identity,
                                      bias=inb_sb[:, cch:cch + 1])
                        cacu = p.inc("ac", ac.drain())
                        ev_ac_u[nU] = cacu
                        p.wait(sy, "ac", cacu)
                        ev_st_u[nU] = (stq, p.dma(sy, stq, uvg_fm[
                            cch * 128:(cch + 1) * 128,
                            q0 + nt4 * 512:q0 + (nt4 + 1) * 512], ub[:]))
                        nU += 1
            ev_ve_prev = p.tok("ve")
            ev_te_q = p.tok("te")
        barrier(p)
        phA.close()

        phB = ExitStack()
        sbB = lambda name, shape, dt: phB.enter_context(nc.sbuf_tensor(name, shape, dt))
        u_bf = [sbB("u_bf%d" % i, [128, TOK], BF16) for i in range(2)]
        v_bf = [sbB("v_bf%d" % i, [128, TOK], BF16) for i in range(2)]
        g_bf = [sbB("g_bf%d" % i, [128, TOK], BF16) for i in range(2)]
        u_f32 = sbB("u_f32", [128, TOK], F32)
        ua_bf = sbB("ua_bf", [128, TOK], BF16)
        diag_sb = sbB("diag_sb", [128, TAPS, 128], BF16)
        sgt = sbB("sgt", [128, 512], F32)
        svt = sbB("svt", [128, 512], F32)
        ygt = sbB("ygt", [128, 512], BF16)
        yf32 = sbB("yf32", [128, 512], F32)
        ev_gate_prev = 0
        ld_tok = {}
        cve_conv = {}
        cac_gate = {}
        cac_silu = {}
        cte_taps = {}

        def issue_uvg_loads(g):
            par = g % 2
            if g - 2 in cve_conv:
                p.wait(sy, "ve", cve_conv[g - 2])
                p.wait(sy, "ac", cac_gate[g - 2])
            tu = p.dma(sy, "ldA", u_bf[par][:], uvg_fm[g * 128:(g + 1) * 128, :])
            tv = p.dma(sy, "ldB", v_bf[par][:],
                       uvg_fm[CIN + g * 128:CIN + (g + 1) * 128, :])
            tg2 = p.dma(sy, "ldH", g_bf[par][:],
                        uvg_fm[2 * CIN + g * 128:2 * CIN + (g + 1) * 128, :])
            ld_tok[g] = (tu, tv, tg2)

        issue_uvg_loads(0)
        for g in range(NG):
            par = g % 2
            if g + 1 < NG:
                issue_uvg_loads(g + 1)
            tu, tv, tg2 = ld_tok[g]
            u_b, v_b, g_b = u_bf[par], v_bf[par], g_bf[par]
            p.wait(ve, "ldA", tu)
            p.wait(ve, "ac", cac_silu.get(g - 1, 0))
            for s in range(BLOC):
                t0 = s * SEQ
                ve.tensor_scalar(u_f32[:, t0:t0 + SEQ], u_b[:, t0:t0 + SEQ],
                                 convw_sb[:, g, 1:2], None, op0=ALU.mult)
                ve.drain()
                ve.scalar_tensor_tensor(u_f32[:, t0 + 1:t0 + SEQ],
                                        u_b[:, t0:t0 + SEQ - 1],
                                        convw_sb[:, g, 0:1],
                                        u_f32[:, t0 + 1:t0 + SEQ],
                                        op0=ALU.mult, op1=ALU.add)
                ve.drain()
                ve.scalar_tensor_tensor(u_f32[:, t0:t0 + SEQ - 1],
                                        u_b[:, t0 + 1:t0 + SEQ],
                                        convw_sb[:, g, 2:3],
                                        u_f32[:, t0:t0 + SEQ - 1],
                                        op0=ALU.mult, op1=ALU.add)
                ve.drain()
            cve_conv[g] = p.inc("ve", ve.drain())
            p.wait(ac, "ve", cve_conv[g])
            ac.activation(u_f32[:], u_f32[:], AF.Silu, bias=convb_sb[:, g:g + 1])
            cac_silu[g] = p.inc("ac", ac.drain())
            p.wait(ve, "ac", cac_silu[g])
            p.wait(ve, "te", cte_taps.get(g - 1, 0))
            p.inc("ve", ve.tensor_copy(ua_bf[:], u_f32[:]))
            for dl in range(TAPS):
                ve.tensor_scalar(diag_sb[:, dl, :], eye_bf[:], ktaps[:, dl, g:g + 1],
                                 None, op0=ALU.mult)
            cve_d = p.inc("ve", ve.drain())
            p.wait(te, "ve", cve_d)
            for s in range(BLOC):
                for nt4 in range(SEQ // 512):
                    t0 = s * SEQ + nt4 * 512
                    tloc = nt4 * 512
                    pb = pA if nt4 % 2 == 0 else pB
                    first = True
                    for dl in range(TAPS):
                        lo = max(0, dl - tloc)
                        if lo >= 512:
                            continue
                        src0 = t0 + lo - dl
                        n = 512 - lo
                        mi = te.matmul(pb[:, lo:512], diag_sb[:, dl, :],
                                       ua_bf[:, src0:src0 + n],
                                       start=first, stop=(dl == TAPS - 1))
                        first = False
                    cte = p.inc("te", mi)
                    p.wait(ac, "ldB", tv)
                    p.wait(ac, "ldH", tg2)
                    p.wait(ac, "te", cte)
                    p.wait(ac, "ve", ev_gate_prev)
                    ac.activation(sgt[:], g_b[:, t0:t0 + 512], AF.Sigmoid)
                    ac.drain()
                    ac.activation(svt[:], v_b[:, t0:t0 + 512], AF.Silu)
                    cac = p.inc("ac", ac.drain())
                    cac_gate[g] = cac
                    p.wait(ve, "ac", cac)
                    p.wait(ve, "te", cte)
                    p.wait(ve, "stR", p.tok("stR"))
                    ve.tensor_mul(yf32[:], pb[:, :], sgt[:])
                    ve.drain()
                    ve.tensor_add(yf32[:], yf32[:], svt[:])
                    ve.drain()
                    ve.tensor_copy(ygt[:], yf32[:])
                    cve = p.inc("ve", ve.drain())
                    ev_gate_prev = cve
                    p.wait(te, "ve", cve)
                    p.wait(sy, "ve", cve)
                    p.dma(sy, "stR", yg_fm[g * 128:(g + 1) * 128, t0:t0 + 512], ygt[:])
            cte_taps[g] = p.tok("te")
        barrier(p)
        phB.close()

        phC = ExitStack()
        sbC = lambda name, shape, dt: phC.enter_context(nc.sbuf_tensor(name, shape, dt))
        outw_sb = sbC("outw_sb", [128, NG, D], BF16)
        for g in range(NG):
            p.dma(gp, "ldG", outw_sb[:, g, :], outw_in[g * 128:(g + 1) * 128, :])
        t_outw = p.tok("ldG")
        yg_sb = [sbC("yg_sb%d" % i, [128, NG, 128], BF16) for i in range(2)]
        hi_fm = sbC("hi_fmC", [128, DCH, 1024], BF16)
        lo_fm = sbC("lo_fmC", [128, DCH, 1024], BF16)
        QT = 1024
        tile_i = 0
        ev_ve_router_prev = p.tok("ve")
        ev_te_qtile = p.tok("te")
        t_yg_tok = {}
        cte_out = {}

        def load_yg(ti):
            if ti >= TOK // 128:
                return
            p.wait(sy, "te", cte_out.get(ti - 2, 0))
            for g in range(NG):
                p.dma(sy, "ldB", yg_sb[ti % 2][:, g, :],
                      yg_fm[g * 128:(g + 1) * 128, ti * 128:(ti + 1) * 128])
            t_yg_tok[ti] = p.tok("ldB")

        load_yg(0)
        for qi in range(TOK // QT):
            q0 = qi * QT
            for tt in range(QT // 128):
                r0 = q0 + tt * 128
                ti = r0 // 128
                load_yg(ti + 1)
                ts = p.dma(sy, "ldA", s_t[:], s2_dram[r0:r0 + 128, :])
                p.wait(te, "ldB", t_yg_tok[ti])
                p.wait(te, "ldG", t_outw)
                p.wait(te, "ve", ev_ve_router_prev)
                for half in range(2):
                    pb = pA if half == 0 else pB
                    for g in range(NG):
                        mi = te.matmul(pb[:, :], yg_sb[ti % 2][:, g, :],
                                       outw_sb[:, g, half * 512:(half + 1) * 512],
                                       start=(g == 0), stop=(g == NG - 1))
                cte = p.inc("te", mi)
                cte_out[ti] = cte
                p.wait(ve, "te", cte)
                p.wait(ve, "ldA", ts)
                p.wait(ve, "stR", p.tok("stR"))
                ve.reduce_sum(sqs[:], s_t[:], axis=AX, negate=True)
                ve.drain()
                ve.tensor_scalar_mul(mcol[:], sqs[:], 1.0 / D)
                ve.drain()
                cve = p.inc("ve", ve.tensor_scalar_add(tmp_t[:], s_t[:], mcol[:, 0:1]))
                p.wait(ac, "ve", cve)
                ac.activation(hi32_t[:], tmp_t[:], AF.Square, accum_out=sqs[:, 0:1])
                ac.drain()
                ac.activation(vcol[:], sqs[:, 0:1], AF.Ln, scale=1.0 / D,
                              bias=eps_col[:, 0:1])
                ac.drain()
                ac.activation(vcol[:], vcol[:], AF.Exp, scale=-0.5)
                cac = p.inc("ac", ac.drain())
                p.wait(ve, "ac", cac)
                ve.tensor_scalar(tmp2_t[:], tmp_t[:], vcol[:, 0:1], None, op0=ALU.mult)
                ve.drain()
                ve.tensor_add(tmp2_t[:], tmp2_t[:], s_t[:])
                ve.drain()
                ve.tensor_add(tmp2_t[:, 0:512], tmp2_t[:, 0:512], pA[:, :])
                ve.drain()
                ve.tensor_add(tmp2_t[:, 512:1024], tmp2_t[:, 512:1024], pB[:, :])
                ve.drain()
                cve = p.inc("ve", ve.tensor_add(s_t[:], tmp2_t[:], outbb[:]))
                p.wait(te, "ve", cve)
                p.wait(sy, "ve", cve)
                p.dma(sy, "stR", s3_out[r0:r0 + 128, :], s_t[:])
                ve.reduce_sum(sqs[:], s_t[:], axis=AX, negate=True)
                ve.drain()
                ve.tensor_scalar_mul(mcol[:], sqs[:], 1.0 / D)
                ve.drain()
                cve = p.inc("ve", ve.tensor_scalar_add(tmp_t[:], s_t[:], mcol[:, 0:1]))
                p.wait(ac, "ve", cve)
                ac.activation(hi32_t[:], tmp_t[:], AF.Square, accum_out=sqs[:, 0:1])
                ac.drain()
                ac.activation(vcol[:], sqs[:, 0:1], AF.Ln, scale=1.0 / D,
                              bias=eps_col[:, 0:1])
                ac.drain()
                ac.activation(vcol[:], vcol[:], AF.Exp, scale=-0.5)
                cac = p.inc("ac", ac.drain())
                p.wait(ve, "ac", cac)
                ve.tensor_scalar(lnbf_t[:], tmp_t[:], vcol[:, 0:1], None, op0=ALU.mult)
                ve.tensor_scalar(tmp_t[:], tmp_t[:], vcol[:, 0:1], None, op0=ALU.mult)
                ve.drain()
                ve.tensor_copy(hi32_t[:], lnbf_t[:])
                ve.drain()
                ve.tensor_sub(tmp_t[:], tmp_t[:], hi32_t[:])
                ve.drain()
                ve.tensor_copy(lo_t[:], tmp_t[:])
                cve = p.inc("ve", ve.drain())
                p.wait(sy, "ve", cve)
                p.dma(sy, "stR", rows_out[r0:r0 + 128, :], lnbf_t[:])
                p.dma(sy, "stR", lo_rows[r0:r0 + 128, :], lo_t[:])
            t_rows_q = p.tok("stR")
            ev_te_outproj = p.tok("te")
            p.wait(sy, "stR", t_rows_q)
            p.wait(sy, "te", ev_te_outproj)
            for j in range(DCH):
                p.dma(sy, "trA", hi_fm[:, j, :],
                      rows_out[q0:q0 + QT, j * 128:(j + 1) * 128], transpose=True)
                p.dma(sy, "trA", lo_fm[:, j, :],
                      lo_rows[q0:q0 + QT, j * 128:(j + 1) * 128], transpose=True)
            t_rfm = p.tok("trA")
            p.wait(te, "trA", t_rfm)
            p.wait(te, "ve", c_wg)
            for tt in range(QT // 128):
                t0 = tt * 128
                p.wait(te, "ve", ev_ve_router_prev)
                for j in range(DCH):
                    te.matmul(pR[:, 0:NEXP], hi_fm[:, j, t0:t0 + 128],
                              wg_hi[:, j, :], start=(j == 0), stop=False)
                for j in range(DCH):
                    te.matmul(pR[:, 0:NEXP], hi_fm[:, j, t0:t0 + 128],
                              wg_lo[:, j, :], start=False, stop=False)
                for j in range(DCH):
                    mi = te.matmul(pR[:, 0:NEXP], lo_fm[:, j, t0:t0 + 128],
                                   wg_hi[:, j, :], start=False, stop=(j == DCH - 1))
                cte = p.inc("te", mi)
                p.wait(ve, "te", cte)
                p.wait(ve, "ac", c_bgb)
                ve.tensor_add(logit_t[:, 0:NEXP], pR[:, 0:NEXP], bgb[:, :])
                cve = p.inc("ve", ve.drain())
                p.wait(te, "ve", cve)
                ve.reduce_max(mx_t[:], logit_t[:, 0:32], axis=AX)
                ve.drain()
                ve.tensor_scalar(eq_t[:], logit_t[:, 0:NEXP], mx_t[:, 0:1],
                                 None, op0=ALU.subtract)
                cve = p.inc("ve", ve.drain())
                p.wait(ac, "ve", cve)
                ac.activation(eq_t[:], eq_t[:], AF.Exp, accum_out=ssum_t[:, 0:1])
                cac = p.inc("ac", ac.drain())
                p.wait(ve, "ac", cac)
                ve.reciprocal(gate_col[:], ssum_t[:])
                ve.drain()
                ve.tensor_scalar(eq_t[:], logit_t[:, 0:NEXP], mx_t[:, 0:1], None,
                                 op0=ALU.is_ge)
                ve.drain()
                ve.tensor_mul(cand_t[:, 0:NEXP], eq_t[:], iota_me[:])
                ve.drain()
                ve.tensor_scalar_add(cand_t[:, 0:NEXP], cand_t[:, 0:NEXP], 65536.0)
                ve.drain()
                ve.tensor_reduce(idx_col[:], cand_t[:, 0:32], axis=AX, op=ALU.min)
                ve.drain()
                ve.tensor_scalar(ohf_t[:], iota_e[:], idx_col[:, 0:1], None,
                                 op0=ALU.is_equal)
                ve.drain()
                ve.tensor_copy(oh_t[:], ohf_t[:])
                cve = p.inc("ve", ve.drain())
                p.wait(te, "ve", cve)
                te.matmul(pR[:, 0:NEXP], trilbf[:, :], oh_t[:, :], start=True, stop=True)
                te.matmul(pC[:, 0:NEXP], ones_r[:, :], run_tot[:, :],
                          start=True, stop=True)
                cte2 = p.inc("te", te.matmul(pG3[0:1, 0:NEXP], ones_k[:, :], ohf_t[:, :],
                                             start=True, stop=True))
                p.wait(ac, "te", cte2)
                ac.copy(runb[:, :], pC[:, 0:NEXP])
                cac = p.inc("ac", ac.drain())
                p.wait(ve, "te", cte2)
                p.wait(ve, "ac", cac)
                ve.tensor_add(gsel_t[:, 0:NEXP], pR[:, 0:NEXP], runb[:, :])
                ve.drain()
                ve.tensor_mul(gsel_t[:, 0:NEXP], gsel_t[:, 0:NEXP], ohf_t[:])
                ve.drain()
                ve.tensor_reduce(lpos_col[:], gsel_t[:, 0:32], axis=AX, op=ALU.add)
                ve.drain()
                ve.tensor_add(run_tot[:, :], run_tot[:, :], pG3[0:1, 0:NEXP])
                ve.drain()
                ve.tensor_copy(idx_acc[:, tile_i:tile_i + 1], idx_col[:])
                ve.tensor_copy(gate_acc[:, tile_i:tile_i + 1], gate_col[:])
                ve.tensor_copy(lpos_acc[:, tile_i:tile_i + 1], lpos_col[:])
                cve = p.inc("ve", ve.drain())
                ev_ve_router_prev = cve
                p.wait(te, "ve", cve)
                ev_te_qtile = p.tok("te")
                tile_i += 1
        cve = p.tok("ve")
        p.wait(sy, "ve", cve)
        p.dma(sy, "stR", idx_out[:, :], idx_acc[:, :])
        p.dma(sy, "stR", gate_out[:, :], gate_acc[:, :])
        p.dma(sy, "stR", lpos_out[:, :], lpos_acc[:, :])
        p.dma(sy, "stR", cnt_out[:, :], run_tot[:, :])
        p.wait(sy, "stR")
        phC.close()
    return nc


def build_L5():
    nc = bass.Bass("TRN2", target_bir_lowering=False, debug=False,
                   detect_race_conditions=False)
    st = ExitStack()
    s3_in = nc.dram_tensor("s3", [TOK, D], F32, kind="ExternalInput").ap()
    yall_in = nc.dram_tensor("y2_all", [TOKALL, D], BF16, kind="ExternalInput").ap()
    destm_in = nc.dram_tensor("dest2_mine", [128, NT], F32, kind="ExternalInput").ap()
    gatem_in = nc.dram_tensor("gate2_mine", [128, NT], F32, kind="ExternalInput").ap()
    out_out = nc.dram_tensor("final", [TOK, D], F32, kind="ExternalOutput").ap()
    gp = nc.gpsimd
    with st:
        p = P(nc, st, ("gp", "te", "ve", "ac", "ldA", "ldB", "stR", "scat"))
        sba = lambda name, shape, dt: st.enter_context(nc.sbuf_tensor(name, shape, dt))
        bc_reg = gp.to_reg(TOKALL - 1)
        _emit_gather_add(nc, p, sba, yall_in, destm_in, gatem_in, s3_in,
                         out_out, bc_reg)
        p.wait(nc.sync, "stR")
    return nc



_CACHE = {}
_DBG = {}


def _get(name, builder):
    if name not in _CACHE:
        _CACHE[name] = builder()
    return _CACHE[name]


def _cores():
    return list(range(NCORE))


def kernel(L0_attn_a_x, L0_attn_a_context, L0_attn_a_Wq, L0_attn_a_Wk,
           L0_attn_a_Wv, L0_attn_a_Wout, L0_attn_a_bout, L0_attn_b_norm_gamma,
           L0_attn_b_norm_beta, L0_attn_b_in_w, L0_attn_b_in_b, L0_attn_b_conv_w,
           L0_attn_b_conv_b, L0_attn_b_A_log, L0_attn_b_B_p, L0_attn_b_C_p,
           L0_attn_b_D_p, L0_attn_b_out_w, L0_attn_b_out_b, L0_ffn_Wg,
           L0_ffn_bg, L0_ffn_W1, L0_ffn_b1, L0_ffn_W2, L0_ffn_b2):
    f32 = lambda a: np.ascontiguousarray(np.asarray(a), dtype=np.float32)
    x = f32(L0_attn_a_x).reshape(BATCH * SEQ, D)
    ctx = f32(L0_attn_a_context).reshape(CTXT, CD)
    wq, wk, wv, wo = map(f32, (L0_attn_a_Wq, L0_attn_a_Wk, L0_attn_a_Wv, L0_attn_a_Wout))
    bout = f32(L0_attn_a_bout).reshape(1, D)
    wg = f32(L0_ffn_Wg)
    bg = f32(L0_ffn_bg).reshape(1, NEXP)
    w1 = f32(L0_ffn_W1)
    b1 = f32(L0_ffn_b1)
    w2 = f32(L0_ffn_W2)
    b2 = f32(L0_ffn_b2)

    nc1 = _get("L1", build_L1)
    im = [dict(x=x[c * TOK:(c + 1) * TOK], ctx=ctx, wq=wq, wk=wk, wv=wv, wo=wo,
               bout=bout, wg=wg, bg=bg) for c in _cores()]
    r1 = run_bass_kernel_spmd(nc1, im, core_ids=_cores()).results
    s1 = [r["s1"] for r in r1]
    rows_all = np.concatenate([r["rows"] for r in r1], axis=0)
    idx_all = np.concatenate([r["idx"] for r in r1], axis=1)
    lpos_all = np.concatenate([r["lpos"] for r in r1], axis=1)
    gate_all = np.concatenate([r["gate"] for r in r1], axis=1)
    cnt_all = np.concatenate([r["cnt"] for r in r1], axis=0)

    l8 = np.tril(np.ones((8, 8), np.float32), -1).T.copy()
    nc2 = _get("L2", build_L2)

    def ffn_launch(rows_a, idx_a, lpos_a, cnt_a):
        ims = []
        for c in _cores():
            es = slice(c * EXP_PC, (c + 1) * EXP_PC)
            b1c = b1[es].reshape(EXP_PC, HID // 128, 128).transpose(0, 2, 1).copy()
            ims.append(dict(rows_all=rows_a, idx_all=idx_a, lpos_all=lpos_a,
                            cnt_all=cnt_a, w1=w1[es].copy(), w2=w2[es].copy(),
                            b1=b1c, b2=b2[es][:, None, :].copy(), l8=l8,
                            mybase=np.full((128, 1), c * SLOTS, np.float32)))
        rr = run_bass_kernel_spmd(nc2, ims, core_ids=_cores()).results
        y_all = np.concatenate([r["y_slice"] for r in rr], axis=0)
        dest_all = rr[0]["dest_all"]
        return y_all, dest_all

    y1_all, dest1_all = ffn_launch(rows_all, idx_all, lpos_all, cnt_all)
    _DBG.update(s1=s1, rows_all=rows_all, idx_all=idx_all, lpos_all=lpos_all,
                gate_all=gate_all, cnt_all=cnt_all, y1_all=y1_all,
                dest1_all=dest1_all)

    nc3 = _get("L3", build_L3)
    in_w = f32(L0_attn_b_in_w)
    in_b_pc = f32(L0_attn_b_in_b).reshape(3 * CIN // 128, 128).T.copy()
    convw_pc = f32(L0_attn_b_conv_w)[:, 0, :].reshape(CIN // 128, 128, 3)\
        .transpose(1, 0, 2).copy()
    convb_pc = f32(L0_attn_b_conv_b).reshape(CIN // 128, 128).T.copy()
    alog_pc = f32(L0_attn_b_A_log).reshape(CIN // 128, 128, DST)\
        .transpose(1, 0, 2).reshape(128, -1).copy()
    bp_pc = f32(L0_attn_b_B_p).reshape(CIN // 128, 128, DST)\
        .transpose(1, 0, 2).reshape(128, -1).copy()
    cp_pc = f32(L0_attn_b_C_p).reshape(CIN // 128, 128, DST)\
        .transpose(1, 0, 2).reshape(128, -1).copy()
    dp_pc = f32(L0_attn_b_D_p).reshape(CIN // 128, 128).T.copy()
    out_w = f32(L0_attn_b_out_w)
    out_b = f32(L0_attn_b_out_b).reshape(1, D)
    gam = f32(L0_attn_b_norm_gamma).reshape(1, D)
    bet = f32(L0_attn_b_norm_beta).reshape(1, D)
    ims = []
    for c in _cores():
        cs = slice(c * NT, (c + 1) * NT)
        ims.append(dict(s1=s1[c], y_all=y1_all, dest_mine=dest1_all[:, cs].copy(),
                        gate_mine=gate_all[:, cs].copy(), in_w=in_w, in_b=in_b_pc,
                        conv_w=convw_pc, conv_b=convb_pc, a_log=alog_pc, b_p=bp_pc,
                        c_p=cp_pc, d_p=dp_pc, out_w=out_w, out_b=out_b, gam=gam,
                        bet=bet, wg=wg, bg=bg))
    r3 = run_bass_kernel_spmd(nc3, ims, core_ids=_cores()).results
    s3 = [r["s3"] for r in r3]
    rows2_all = np.concatenate([r["rows2"] for r in r3], axis=0)
    idx2_all = np.concatenate([r["idx2"] for r in r3], axis=1)
    lpos2_all = np.concatenate([r["lpos2"] for r in r3], axis=1)
    gate2_all = np.concatenate([r["gate2"] for r in r3], axis=1)
    cnt2_all = np.concatenate([r["cnt2"] for r in r3], axis=0)

    _DBG.update(s3=s3, rows2_all=rows2_all, idx2_all=idx2_all,
                lpos2_all=lpos2_all, gate2_all=gate2_all, cnt2_all=cnt2_all)
    y2_all, dest2_all = ffn_launch(rows2_all, idx2_all, lpos2_all, cnt2_all)
    _DBG.update(y2_all=y2_all, dest2_all=dest2_all)

    nc5 = _get("L5", build_L5)
    ims = []
    for c in _cores():
        cs = slice(c * NT, (c + 1) * NT)
        ims.append(dict(s3=s3[c], y2_all=y2_all, dest2_mine=dest2_all[:, cs].copy(),
                        gate2_mine=gate2_all[:, cs].copy()))
    r5 = run_bass_kernel_spmd(nc5, ims, core_ids=_cores()).results
    out = np.concatenate([r["final"] for r in r5], axis=0)
    return out.reshape(BATCH, SEQ, D)


# version 05 (final)
# speedup vs baseline: 1.0080x; 1.0080x over previous
import numpy as np
from contextlib import ExitStack

import concourse.bass as bass
import concourse.mybir as mybir
from concourse.bass import IndirectOffsetOnAxis
from concourse.bass_utils import run_bass_kernel_spmd

F32 = mybir.dt.float32
BF16 = mybir.dt.bfloat16
U32 = mybir.dt.uint32
AF = mybir.ActivationFunctionType
ALU = mybir.AluOpType
AX = mybir.AxisListType.X

NCORE = 8
BATCH, SEQ, D = 32, 2048, 1024
BLOC = BATCH // NCORE
TOK = BLOC * SEQ
NT = TOK // 128
DCH = D // 128
CTXB, CTXS, CD = 2, 2048, 768
CTXT = CTXB * CTXS
CCH = CD // 128
HEADS, DH = 8, 128
SCALE_Q = 1.5 * DH ** -0.5
NEXP, CAP = 16, 4096
HID = 4096
EXP_PC = NEXP // NCORE
SLOTS = EXP_PC * CAP
CIN = 2048
DST = 64
TAPS = 24
EPS = 1e-5
BIGF = 1.0e9


class P(object):

    def __init__(self, nc, stack, names):
        self.nc = nc
        self.sem = {}
        self.cnt = {}
        self.waited = {}
        for n in names:
            self.sem[n] = stack.enter_context(nc.semaphore("sem_" + n))
            self.cnt[n] = 0

    def inc(self, key, instr, n=1):
        instr.then_inc(self.sem[key], n)
        self.cnt[key] += n
        return self.cnt[key]

    def tok(self, key):
        return self.cnt[key]

    def wait(self, eng, key, target=None):
        if target is None:
            target = self.cnt[key]
        if target <= 0:
            return
        k = (id(eng), key)
        if target > self.waited.get(k, 0):
            eng.wait_ge(self.sem[key], target)
            self.waited[k] = target

    def dma(self, eng, key, out, in_, **kw):
        return self.inc(key, eng.dma_start(out=out, in_=in_, **kw), 16)


def build_L1(nqt=None, stage=99):
    nc = bass.Bass("TRN2", target_bir_lowering=False, debug=False,
                   detect_race_conditions=False)
    st = ExitStack()
    x_in = nc.dram_tensor("x", [TOK, D], F32, kind="ExternalInput").ap()
    ctx_in = nc.dram_tensor("ctx", [CTXT, CD], F32, kind="ExternalInput").ap()
    wq_in = nc.dram_tensor("wq", [D, D], F32, kind="ExternalInput").ap()
    wk_in = nc.dram_tensor("wk", [CD, D], F32, kind="ExternalInput").ap()
    wv_in = nc.dram_tensor("wv", [CD, D], F32, kind="ExternalInput").ap()
    wo_in = nc.dram_tensor("wo", [D, D], F32, kind="ExternalInput").ap()
    bout_in = nc.dram_tensor("bout", [1, D], F32, kind="ExternalInput").ap()
    wg_in = nc.dram_tensor("wg", [D, NEXP], F32, kind="ExternalInput").ap()
    bg_in = nc.dram_tensor("bg", [1, NEXP], F32, kind="ExternalInput").ap()

    s1_out = nc.dram_tensor("s1", [TOK, D], F32, kind="ExternalOutput").ap()
    rows_out = nc.dram_tensor("rows", [TOK, D], BF16, kind="ExternalOutput").ap()
    idx_out = nc.dram_tensor("idx", [128, NT], F32, kind="ExternalOutput").ap()
    gate_out = nc.dram_tensor("gate", [128, NT], F32, kind="ExternalOutput").ap()
    lpos_out = nc.dram_tensor("lpos", [128, NT], F32, kind="ExternalOutput").ap()
    cnt_out = nc.dram_tensor("cnt", [1, NEXP], F32, kind="ExternalOutput").ap()

    lo_rows = nc.dram_tensor("lo_rows", [TOK, D], BF16, kind="Internal").ap()
    lnx_rows = nc.dram_tensor("lnx_rows", [TOK, D], BF16, kind="Internal").ap()
    v_dram = nc.dram_tensor("v_dram", [CTXT, D], BF16, kind="Internal").ap()
    ctx_bf = nc.dram_tensor("ctx_bf", [CTXT, CD], BF16, kind="Internal").ap()

    te, ve, ac, gp, sy = nc.tensor, nc.vector, nc.scalar, nc.gpsimd, nc.sync

    with st:
        p = P(nc, st, ("gp", "te", "ve", "ac", "ldA", "ldB", "ldG", "stR", "trA", "dbV"))
        sb = lambda name, shape, dt: st.enter_context(nc.sbuf_tensor(name, shape, dt))
        ps = lambda name, shape: st.enter_context(nc.psum_tensor(name, shape, F32))

        k_fm = sb("k_fm", [128, HEADS, CTXT], BF16)
        wq_sb = sb("wq_sb", [128, DCH, D], BF16)
        wo_sb = sb("wo_sb", [128, HEADS, D], BF16)
        wg_hi = sb("wg_hi", [128, DCH, NEXP], BF16)
        wg_lo = sb("wg_lo", [128, DCH, NEXP], BF16)
        boutb = sb("boutb", [128, D], F32)
        bg_row = sb("bg_row", [1, NEXP], F32)
        ones_r = sb("ones_r", [1, 128], F32)
        ones_k = sb("ones_k", [128, 1], F32)
        trilbf = sb("trilbf", [128, 128], BF16)
        iota_e = sb("iota_e", [128, NEXP], F32)
        iota_me = sb("iota_me", [128, NEXP], F32)
        run_tot = sb("run_tot", [1, NEXP], F32)
        mcol = sb("mcol", [128, 1], F32)
        eps_col = sb("eps_col", [128, 1], F32)
        sqs = sb("sqs", [128, 1], F32)
        vcol = sb("vcol", [128, 1], F32)
        bgb = sb("bgb", [128, NEXP], F32)
        runb = sb("runb", [128, NEXP], F32)
        idx_acc = sb("idx_acc", [128, NT], F32)
        gate_acc = sb("gate_acc", [128, NT], F32)
        lpos_acc = sb("lpos_acc", [128, NT], F32)

        logit_t = sb("logit_t", [128, 32], F32)
        mx_t = sb("mx_t", [128, 1], F32)
        nmx_t = sb("nmx_t", [128, 1], F32)
        ssum_t = sb("ssum_t", [128, 1], F32)
        gate_col = sb("gate_col", [128, 1], F32)
        eq_t = sb("eq_t", [128, NEXP], F32)
        cand_t = sb("cand_t", [128, 32], F32)
        idx_col = sb("idx_col", [128, 1], F32)
        oh_t = sb("oh_t", [128, NEXP], BF16)
        ohf_t = sb("ohf_t", [128, NEXP], F32)
        gsel_t = sb("gsel_t", [128, 32], F32)
        lpos_col = sb("lpos_col", [128, 1], F32)

        pA = ps("pA", [128, 512])
        pB = ps("pB", [128, 512])
        pC = ps("pC", [128, 512])
        pD = ps("pD", [128, 512])
        pE = ps("pE", [128, 512])
        pF = ps("pF", [128, 512])
        pG = ps("pG", [128, 512])
        pR = ps("pR", [128, 512])

        tr = ExitStack()
        sbt = lambda name, shape, dt: tr.enter_context(nc.sbuf_tensor(name, shape, dt))
        wk_sb = sbt("wk_sb", [128, CCH, D], BF16)
        wv_sb = sbt("wv_sb", [128, CCH, D], BF16)
        ctx_fm = sbt("ctx_fm", [128, CCH, CTXT], BF16)
        v_t = sbt("v_t", [128, D], BF16)
        bout_sb = sbt("bout_sb", [1, D], F32)
        wg32 = sbt("wg32", [128, DCH, NEXP], F32)
        wg_t32 = sbt("wg_t32", [128, DCH, NEXP], F32)
        tril_f = sbt("tril_f", [128, 128], F32)
        iota_rf = sbt("iota_rf", [128, 128], F32)

        gp.iota(iota_e[:], pattern=[[1, NEXP]], base=0, channel_multiplier=0,
                allow_small_or_imprecise_dtypes=True)
        gp.iota(iota_rf[:], pattern=[[1, 128]], base=0, channel_multiplier=0,
                allow_small_or_imprecise_dtypes=True)
        c_iota = p.inc("gp", gp.iota(mcol[:], pattern=[[0, 1]], base=0,
                                     channel_multiplier=1,
                                     allow_small_or_imprecise_dtypes=True))
        p.wait(ve, "gp", c_iota)
        ve.memset(ones_r[:], 1.0)
        ve.memset(eps_col[:], EPS)
        ve.memset(logit_t[:, 16:32], -1.0e30)
        ve.memset(cand_t[:, 16:32], 1.0e30)
        ve.memset(gsel_t[:, 16:32], 0.0)
        ve.memset(ones_k[:], 1.0)
        ve.memset(run_tot[:], 0.0)
        ve.tensor_scalar_add(iota_me[:], iota_e[:], -65536.0)
        ve.tensor_scalar(tril_f[:], iota_rf[:], mcol[:, 0:1], None, op0=ALU.is_gt)
        ve.drain()
        c_tril = p.inc("ve", ve.tensor_copy(trilbf[:], tril_f[:]))

        for j in range(CCH):
            p.dma(gp, "ldG", wk_sb[:, j, :], wk_in[j * 128:(j + 1) * 128, :])
            p.dma(gp, "ldG", wv_sb[:, j, :], wv_in[j * 128:(j + 1) * 128, :])
        p.dma(gp, "ldG", ctx_bf[:, :], ctx_in[:, :])
        t_wkv = p.tok("ldG")
        for j in range(DCH):
            p.dma(gp, "ldG", wq_sb[:, j, :], wq_in[j * 128:(j + 1) * 128, :])
        for h in range(HEADS):
            p.dma(gp, "ldG", wo_sb[:, h, :], wo_in[h * 128:(h + 1) * 128, :])
        t_wq = p.tok("ldG")
        p.dma(sy, "ldA", bout_sb[:], bout_in)
        p.dma(sy, "ldA", bg_row[:], bg_in)
        for j in range(DCH):
            p.dma(sy, "ldA", wg32[:, j, :], wg_in[j * 128:(j + 1) * 128, :])
        t_misc = p.tok("ldA")
        p.wait(ve, "ldA", t_misc)
        ve.tensor_copy(wg_hi[:, :, :], wg32[:, :, :])
        ve.drain()
        ve.tensor_copy(wg_t32[:, :, :], wg_hi[:, :, :])
        ve.drain()
        ve.tensor_sub(wg_t32[:, :, :], wg32[:, :, :], wg_t32[:, :, :])
        ve.drain()
        c_wg = p.inc("ve", ve.tensor_copy(wg_lo[:, :, :], wg_t32[:, :, :]))

        if stage < 2:
            p.wait(sy, "stR")
            p.wait(sy, "ldG")
            p.wait(sy, "ldA")
            p.wait(sy, "ve", p.tok("ve"))
            tr.close()
            return nc
        p.wait(sy, "ldG", t_wkv)
        for j in range(CCH):
            p.dma(sy, "trA", ctx_fm[:, j, :], ctx_bf[:, j * 128:(j + 1) * 128],
                  transpose=True)
        t_ctxfm = p.tok("trA")

        p.wait(te, "ldA", t_misc)
        p.wait(te, "ve", c_tril)
        ev_bout = []
        for half in range(2):
            cte = p.inc("te", te.matmul(pR[:, :], ones_r[:, :],
                                        bout_sb[:, half * 512:(half + 1) * 512],
                                        start=True, stop=True))
            p.wait(ac, "te", cte)
            cac = p.inc("ac", ac.copy(boutb[:, half * 512:(half + 1) * 512], pR[:, :]))
            ev_bout.append(cac)
            p.wait(te, "ac", cac)
        c_boutb = ev_bout[-1]
        cte = p.inc("te", te.matmul(pR[:, 0:NEXP], ones_r[:, :], bg_row[:, :],
                                    start=True, stop=True))
        p.wait(ac, "te", cte)
        c_bgb = p.inc("ac", ac.copy(bgb[:, :], pR[:, 0:NEXP]))
        p.wait(te, "ac", c_bgb)

        if stage < 3:
            p.wait(sy, "trA")
            p.wait(sy, "ac", p.tok("ac"))
            tr.close()
            return nc
        p.wait(te, "trA", t_ctxfm)
        ev_ac_k = {}
        nK = 0
        for h in range(HEADS):
            for kt in range(CTXT // 512):
                pb = pA if (nK % 2 == 0) else pB
                if nK >= 2:
                    p.wait(te, "ac", ev_ac_k[nK - 2])
                for j in range(CCH):
                    mi = te.matmul(pb[:, :], wk_sb[:, j, h * 128:(h + 1) * 128],
                                   ctx_fm[:, j, kt * 512:(kt + 1) * 512],
                                   start=(j == 0), stop=(j == CCH - 1))
                cte = p.inc("te", mi)
                p.wait(ac, "te", cte)
                ev_ac_k[nK] = p.inc("ac", ac.copy(k_fm[:, h, kt * 512:(kt + 1) * 512],
                                                  pb[:, :]))
                nK += 1

        if stage < 4:
            p.wait(sy, "ac", p.tok("ac"))
            tr.close()
            return nc
        ev_st_v = {}
        for kt in range(CTXT // 128):
            if kt >= 1:
                p.wait(ac, "stR", ev_st_v[kt - 1])
            for half in range(2):
                pb = pC if half == 0 else pD
                for j in range(CCH):
                    mi = te.matmul(pb[:, :], ctx_fm[:, j, kt * 128:(kt + 1) * 128],
                                   wv_sb[:, j, half * 512:(half + 1) * 512],
                                   start=(j == 0), stop=(j == CCH - 1))
            cte = p.inc("te", mi)
            p.wait(ac, "te", cte)
            ac.copy(v_t[:, 0:512], pC[:, :])
            cac = p.inc("ac", ac.copy(v_t[:, 512:1024], pD[:, :]))
            p.wait(te, "ac", cac)
            p.wait(sy, "ac", cac)
            ev_st_v[kt] = p.dma(sy, "stR", v_dram[kt * 128:(kt + 1) * 128, :], v_t[:])
        t_vdone = p.tok("stR")

        tr.close()
        for eng in (ve, ac, te, gp, sy):
            for key in ("trA", "stR", "ldA", "ldG", "te", "ve", "ac", "gp"):
                p.wait(eng, key)
        x_t = [sb("x_t%d" % i, [128, D], F32) for i in range(2)]
        tmp_t = sb("tmp_t", [128, D], F32)
        lnbf_t = sb("lnbf_t", [128, D], BF16)
        lnx_fm = sb("lnx_fm", [128, DCH, 1024], BF16)
        q_fm = sb("q_fm", [128, HEADS, 1024], BF16)
        v_buf = [sb("v_buf%d" % i, [128, 32, DH], BF16) for i in range(2)]
        exps = [sb("exps%d" % i, [128, 1024], BF16) for i in range(2)]
        densum = sb("densum", [128, 1024], F32)
        den_row = sb("den_row", [1, 1024], F32)
        rdenb = sb("rdenb", [128, 1024], F32)
        attn_fm = sb("attn_fm", [128, HEADS, 1024], BF16)
        s1_t = sb("s1_t", [128, D], F32)
        hi32_t = sb("hi32_t", [128, D], F32)
        lo_t = sb("lo_t", [128, D], BF16)
        hi_fm = lnx_fm
        lo_fm = q_fm
        p.inc("ve", ve.memset(densum[:], 0.0))

        QT = 1024
        NQT = TOK // QT if nqt is None else nqt
        ev_ve_attn = 0
        cac_rdenb = 0
        ev_evac = {0: 0, 1: 0}
        ev_pv_prev = 0
        tile_i = 0
        ev_ve_router_prev = p.tok("ve")
        ev_te_qtile = p.tok("te")
        p.wait(sy, "stR", t_vdone)
        p.wait(sy, "te", ev_te_qtile)
        p.wait(sy, "ve", ev_ve_router_prev)
        for qi in range(NQT):
            q0 = qi * QT
            p.wait(sy, "ve", ev_ve_router_prev)
            for tt in range(QT // 128):
                r0 = q0 + tt * 128
                xt = x_t[tt % 2]
                tx = p.dma(sy, "ldA", xt[:], x_in[r0:r0 + 128, :])
                p.wait(ve, "ldA", tx)
                p.wait(ve, "stR")
                ve.reduce_sum(sqs[:], xt[:], axis=AX, negate=True)
                ve.drain()
                ve.tensor_scalar_mul(mcol[:], sqs[:], 1.0 / D)
                ve.drain()
                cve = p.inc("ve", ve.tensor_scalar_add(tmp_t[:], xt[:], mcol[:, 0:1]))
                p.wait(ac, "ve", cve)
                ac.activation(hi32_t[:], tmp_t[:], AF.Square, accum_out=sqs[:, 0:1])
                ac.drain()
                ac.activation(vcol[:], sqs[:, 0:1], AF.Ln, scale=1.0 / D, bias=eps_col[:, 0:1])
                ac.drain()
                ac.activation(vcol[:], vcol[:], AF.Exp, scale=-0.5)
                cac = p.inc("ac", ac.drain())
                p.wait(ve, "ac", cac)
                ve.tensor_scalar(lnbf_t[:], tmp_t[:], vcol[:, 0:1],
                                 None, op0=ALU.mult)
                cve = p.inc("ve", ve.drain())
                p.wait(sy, "ve", cve)
                p.dma(sy, "stR", lnx_rows[r0:r0 + 128, :], lnbf_t[:])
            t_lnq = p.tok("stR")
            p.wait(sy, "stR", t_lnq)
            p.wait(sy, "te", ev_te_qtile)
            for j in range(DCH):
                p.dma(sy, "trA", lnx_fm[:, j, :],
                      lnx_rows[q0:q0 + QT, j * 128:(j + 1) * 128], transpose=True)
            t_fmq = p.tok("trA")
            p.wait(te, "trA", t_fmq)
            p.wait(te, "ldG", t_wq)
            p.wait(te, "ve", ev_ve_attn)
            for h in range(HEADS):
                for half in range(2):
                    pb = pA if half == 0 else pB
                    for j in range(DCH):
                        mi = te.matmul(pb[:, :], wq_sb[:, j, h * 128:(h + 1) * 128],
                                       lnx_fm[:, j, half * 512:(half + 1) * 512],
                                       start=(j == 0), stop=(j == DCH - 1))
                cte = p.inc("te", mi)
                p.wait(ac, "te", cte)
                ac.activation(q_fm[:, h, 0:512], pA[:, :], AF.Copy, scale=SCALE_Q)
                cac = p.inc("ac", ac.activation(q_fm[:, h, 512:1024], pB[:, :],
                                                AF.Copy, scale=SCALE_Q))
                p.wait(te, "ac", cac)
            p.wait(gp, "stR", t_vdone)
            p.wait(gp, "te", ev_pv_prev)
            for kc in range(32):
                p.dma(gp, "dbV", v_buf[0][:, kc, :],
                      v_dram[kc * 128:(kc + 1) * 128, 0:DH])
            ev_pv_head = {}
            for h in range(HEADS):
                t_vh = p.tok("dbV")
                if h + 1 < HEADS:
                    if h >= 1:
                        p.wait(gp, "te", ev_pv_head[h - 1])
                    for kc in range(32):
                        p.dma(gp, "dbV", v_buf[(h + 1) % 2][:, kc, :],
                              v_dram[kc * 128:(kc + 1) * 128, (h + 1) * DH:(h + 2) * DH])
                vb = v_buf[h % 2]
                dsum = densum
                pPV0, pPV1 = (pE, pF) if h % 2 == 0 else (pG, pR)
                p.wait(te, "dbV", t_vh)
                cacs, cpvs, cves = [], [], []

                def qk_banks(kc):
                    if kc < 4:
                        return pB, pD
                    return (pA, pC) if kc % 2 == 0 else (pB, pD)

                def emit_pv(kc):
                    exb = exps[kc % 2]
                    p.wait(te, "ac", cacs[kc])
                    if kc == 0:
                        p.wait(te, "ve", ev_evac[h % 2])
                    te.matmul(pPV0[:, :], vb[:, kc, :], exb[:, 0:512],
                              start=(kc == 0), stop=(kc == 31))
                    cpvs.append(p.inc("te", te.matmul(pPV1[:, :], vb[:, kc, :],
                                                      exb[:, 512:1024],
                                                      start=(kc == 0),
                                                      stop=(kc == 31))))

                for kc in range(32):
                    pS, pS2 = qk_banks(kc)
                    if kc == 4:
                        p.wait(te, "ac", cac_rdenb)
                    elif 1 <= kc <= 3:
                        p.wait(te, "ac", cacs[kc - 1])
                    elif kc >= 5:
                        p.wait(te, "ac", cacs[kc - 2])
                    te.matmul(pS[:, :], k_fm[:, h, kc * 128:(kc + 1) * 128],
                              q_fm[:, h, 0:512], start=True, stop=True)
                    cte = p.inc("te", te.matmul(pS2[:, :],
                                                k_fm[:, h, kc * 128:(kc + 1) * 128],
                                                q_fm[:, h, 512:1024],
                                                start=True, stop=True))
                    if kc >= 1:
                        emit_pv(kc - 1)
                    p.wait(ac, "te", cte)
                    if kc >= 2:
                        p.wait(ac, "te", cpvs[kc - 2])
                        p.wait(ac, "ve", cves[kc - 2])
                    exb = exps[kc % 2]
                    ac.activation(exb[:, 0:512], pS[:, :], AF.Exp)
                    cac = p.inc("ac", ac.activation(exb[:, 512:1024], pS2[:, :], AF.Exp))
                    cacs.append(cac)
                    p.wait(ve, "ac", cac)
                    cves.append(p.inc("ve", ve.tensor_add(dsum[:], dsum[:], exb[:])))
                emit_pv(31)
                ev_pv_head[h] = cpvs[-1]
                p.wait(te, "ve", cves[-1])
                te.matmul(pA[0:1, 0:512], ones_k[:, :], dsum[:, 0:512],
                          start=True, stop=True)
                cte_den = p.inc("te", te.matmul(pC[0:1, 0:512], ones_k[:, :],
                                                dsum[:, 512:1024],
                                                start=True, stop=True))
                p.wait(ve, "te", cte_den)
                ve.tensor_copy(den_row[0:1, 0:512], pA[0:1, 0:512])
                ve.tensor_copy(den_row[0:1, 512:1024], pC[0:1, 0:512])
                ve.drain()
                ve.reciprocal(den_row[:, :], den_row[:, :])
                cve_denrow = p.inc("ve", ve.drain())
                ve.memset(dsum[:], 0.0)
                p.wait(te, "ve", cve_denrow)
                te.matmul(pA[:, :], ones_r[:, :], den_row[0:1, 0:512],
                          start=True, stop=True)
                cte_b = p.inc("te", te.matmul(pC[:, :], ones_r[:, :],
                                              den_row[0:1, 512:1024],
                                              start=True, stop=True))
                p.wait(ac, "te", cte_b)
                p.wait(ac, "ve", ev_ve_attn)
                ac.copy(rdenb[:, 0:512], pA[:, :])
                cac_rdenb = p.inc("ac", ac.copy(rdenb[:, 512:1024], pC[:, :]))
                p.wait(ve, "ac", cac_rdenb)
                ve.tensor_mul(attn_fm[:, h, 0:512], pPV0[:, :], rdenb[:, 0:512])
                ev_ve_attn = p.inc("ve", ve.tensor_mul(attn_fm[:, h, 512:1024],
                                                       pPV1[:, :], rdenb[:, 512:1024]))
                ev_evac[h % 2] = ev_ve_attn
            ev_pv_prev = ev_pv_head[HEADS - 1]
            p.wait(te, "ve", ev_ve_attn)
            p.wait(te, "ac", cac_rdenb)
            for tt in range(QT // 128):
                r0 = q0 + tt * 128
                t0 = tt * 128
                xt = x_t[tt % 2]
                tx = p.dma(sy, "ldB", xt[:], x_in[r0:r0 + 128, :])
                for half in range(2):
                    pb = pA if half == 0 else pB
                    for h in range(HEADS):
                        mi = te.matmul(pb[:, :], attn_fm[:, h, t0:t0 + 128],
                                       wo_sb[:, h, half * 512:(half + 1) * 512],
                                       start=(h == 0), stop=(h == HEADS - 1))
                cte = p.inc("te", mi)
                p.wait(ve, "te", cte)
                p.wait(ve, "ldB", tx)
                p.wait(ve, "ac", c_boutb)
                p.wait(ve, "stR")
                ve.tensor_add(s1_t[:, 0:512], pA[:, :], boutb[:, 0:512])
                ve.tensor_add(s1_t[:, 512:1024], pB[:, :], boutb[:, 512:1024])
                cve = p.inc("ve", ve.tensor_add(s1_t[:], s1_t[:], xt[:]))
                p.wait(te, "ve", cve)
                p.wait(sy, "ve", cve)
                p.dma(sy, "stR", s1_out[r0:r0 + 128, :], s1_t[:])
                ve.reduce_sum(sqs[:], s1_t[:], axis=AX, negate=True)
                ve.drain()
                ve.tensor_scalar_mul(mcol[:], sqs[:], 1.0 / D)
                ve.drain()
                cve = p.inc("ve", ve.tensor_scalar_add(tmp_t[:], s1_t[:], mcol[:, 0:1]))
                p.wait(ac, "ve", cve)
                ac.activation(hi32_t[:], tmp_t[:], AF.Square, accum_out=sqs[:, 0:1])
                ac.drain()
                ac.activation(vcol[:], sqs[:, 0:1], AF.Ln, scale=1.0 / D, bias=eps_col[:, 0:1])
                ac.drain()
                ac.activation(vcol[:], vcol[:], AF.Exp, scale=-0.5)
                cac = p.inc("ac", ac.drain())
                p.wait(ve, "ac", cac)
                ve.tensor_scalar(lnbf_t[:], tmp_t[:], vcol[:, 0:1], None, op0=ALU.mult)
                ve.tensor_scalar(tmp_t[:], tmp_t[:], vcol[:, 0:1], None, op0=ALU.mult)
                ve.drain()
                ve.tensor_copy(hi32_t[:], lnbf_t[:])
                ve.drain()
                ve.tensor_sub(tmp_t[:], tmp_t[:], hi32_t[:])
                ve.drain()
                ve.tensor_copy(lo_t[:], tmp_t[:])
                cve = p.inc("ve", ve.drain())
                p.wait(sy, "ve", cve)
                p.dma(sy, "stR", rows_out[r0:r0 + 128, :], lnbf_t[:])
                p.dma(sy, "stR", lo_rows[r0:r0 + 128, :], lo_t[:])
            t_rows_q = p.tok("stR")
            ev_te_outproj = p.tok("te")
            p.wait(sy, "stR", t_rows_q)
            p.wait(sy, "te", ev_te_outproj)
            for j in range(DCH):
                p.dma(sy, "trA", hi_fm[:, j, :],
                      rows_out[q0:q0 + QT, j * 128:(j + 1) * 128], transpose=True)
                p.dma(sy, "trA", lo_fm[:, j, :],
                      lo_rows[q0:q0 + QT, j * 128:(j + 1) * 128], transpose=True)
            t_rfm = p.tok("trA")
            p.wait(te, "trA", t_rfm)
            p.wait(te, "ve", c_wg)
            for tt in range(QT // 128):
                r0 = q0 + tt * 128
                t0 = tt * 128
                p.wait(te, "ve", ev_ve_router_prev)
                for j in range(DCH):
                    te.matmul(pR[:, 0:NEXP], hi_fm[:, j, t0:t0 + 128],
                              wg_hi[:, j, :], start=(j == 0), stop=False)
                for j in range(DCH):
                    te.matmul(pR[:, 0:NEXP], hi_fm[:, j, t0:t0 + 128],
                              wg_lo[:, j, :], start=False, stop=False)
                for j in range(DCH):
                    mi = te.matmul(pR[:, 0:NEXP], lo_fm[:, j, t0:t0 + 128],
                                   wg_hi[:, j, :], start=False, stop=(j == DCH - 1))
                    if j == DCH - 1:
                        cte = p.inc("te", mi)
                p.wait(ve, "te", cte)
                p.wait(ve, "ac", c_bgb)
                ve.tensor_add(logit_t[:, 0:NEXP], pR[:, 0:NEXP], bgb[:, :])
                cve = p.inc("ve", ve.drain())
                p.wait(te, "ve", cve)
                ve.reduce_max(mx_t[:], logit_t[:, 0:32], axis=AX)
                ve.drain()
                ve.tensor_scalar(eq_t[:], logit_t[:, 0:NEXP], mx_t[:, 0:1],
                                 None, op0=ALU.subtract)
                cve = p.inc("ve", ve.drain())
                p.wait(ac, "ve", cve)
                ac.activation(eq_t[:], eq_t[:], AF.Exp, accum_out=ssum_t[:, 0:1])
                cac = p.inc("ac", ac.drain())
                p.wait(ve, "ac", cac)
                ve.reciprocal(gate_col[:], ssum_t[:])
                ve.drain()
                ve.tensor_scalar(eq_t[:], logit_t[:, 0:NEXP], mx_t[:, 0:1], None, op0=ALU.is_ge)
                ve.drain()
                ve.tensor_mul(cand_t[:, 0:NEXP], eq_t[:], iota_me[:])
                ve.drain()
                ve.tensor_scalar_add(cand_t[:, 0:NEXP], cand_t[:, 0:NEXP], 65536.0)
                ve.drain()
                ve.tensor_reduce(idx_col[:], cand_t[:, 0:32], axis=AX, op=ALU.min)
                ve.drain()
                ve.tensor_scalar(ohf_t[:], iota_e[:], idx_col[:, 0:1], None,
                                 op0=ALU.is_equal)
                ve.drain()
                ve.tensor_copy(oh_t[:], ohf_t[:])
                cve = p.inc("ve", ve.drain())
                p.wait(te, "ve", cve)
                te.matmul(pR[:, 0:NEXP], trilbf[:, :], oh_t[:, :], start=True, stop=True)
                te.matmul(pE[:, 0:NEXP], ones_r[:, :], run_tot[:, :],
                          start=True, stop=True)
                cte2 = p.inc("te", te.matmul(pG[0:1, 0:NEXP], ones_k[:, :], ohf_t[:, :],
                                             start=True, stop=True))
                p.wait(ac, "te", cte2)
                ac.copy(runb[:, :], pE[:, 0:NEXP])
                cac = p.inc("ac", ac.drain())
                p.wait(ve, "te", cte2)
                p.wait(ve, "ac", cac)
                ve.tensor_add(gsel_t[:, 0:NEXP], pR[:, 0:NEXP], runb[:, :])
                ve.drain()
                ve.tensor_mul(gsel_t[:, 0:NEXP], gsel_t[:, 0:NEXP], ohf_t[:])
                ve.drain()
                ve.tensor_reduce(lpos_col[:], gsel_t[:, 0:32], axis=AX, op=ALU.add)
                ve.drain()
                ve.tensor_add(run_tot[:, :], run_tot[:, :], pG[0:1, 0:NEXP])
                ve.drain()
                ve.tensor_copy(idx_acc[:, tile_i:tile_i + 1], idx_col[:])
                ve.tensor_copy(gate_acc[:, tile_i:tile_i + 1], gate_col[:])
                ve.tensor_copy(lpos_acc[:, tile_i:tile_i + 1], lpos_col[:])
                cve = p.inc("ve", ve.drain())
                ev_ve_router_prev = cve
                p.wait(te, "ve", cve)
                ev_te_qtile = p.tok("te")
                tile_i += 1
        cve = p.tok("ve")
        p.wait(sy, "ve", cve)
        p.dma(sy, "stR", idx_out[:, :], idx_acc[:, :])
        p.dma(sy, "stR", gate_out[:, :], gate_acc[:, :])
        p.dma(sy, "stR", lpos_out[:, :], lpos_acc[:, :])
        p.dma(sy, "stR", cnt_out[:, :], run_tot[:, :])
        p.wait(sy, "stR")
    return nc



_CACHE = {}
_DBG = {}


def _get(name, builder):
    if name not in _CACHE:
        _CACHE[name] = builder()
    return _CACHE[name]


def kernel(**inputs):
    raise NotImplementedError("wired up after L2..L5 exist")


TOKALL = TOK * NCORE
NTALL = NT * NCORE


def build_L2():
    nc = bass.Bass("TRN2", target_bir_lowering=False, debug=False,
                   detect_race_conditions=False)
    st = ExitStack()
    rows_in = nc.dram_tensor("rows_all", [TOKALL, D], BF16, kind="ExternalInput").ap()
    idx_in = nc.dram_tensor("idx_all", [128, NTALL], F32, kind="ExternalInput").ap()
    lpos_in = nc.dram_tensor("lpos_all", [128, NTALL], F32, kind="ExternalInput").ap()
    cnt_in = nc.dram_tensor("cnt_all", [8, NEXP], F32, kind="ExternalInput").ap()
    w1_in = nc.dram_tensor("w1", [EXP_PC, D, HID], F32, kind="ExternalInput").ap()
    w2_in = nc.dram_tensor("w2", [EXP_PC, HID, D], F32, kind="ExternalInput").ap()
    b1_in = nc.dram_tensor("b1", [EXP_PC, 128, HID // 128], F32, kind="ExternalInput").ap()
    b2_in = nc.dram_tensor("b2", [EXP_PC, 1, D], F32, kind="ExternalInput").ap()
    l8_in = nc.dram_tensor("l8", [8, 8], F32, kind="ExternalInput").ap()
    mybase_in = nc.dram_tensor("mybase", [128, 1], F32, kind="ExternalInput").ap()

    y_out = nc.dram_tensor("y_slice", [SLOTS, D], BF16, kind="ExternalOutput").ap()
    dest_out = nc.dram_tensor("dest_all", [128, NTALL], F32, kind="ExternalOutput").ap()

    buf_dram = nc.dram_tensor("buf_dram", [SLOTS, D], BF16, kind="Internal").ap()

    te, ve, ac, gp, sy = nc.tensor, nc.vector, nc.scalar, nc.gpsimd, nc.sync
    with st:
        p = P(nc, st, ("gp", "te", "ve", "ac", "ldA", "ldB", "ldG", "ldH",
                       "ldI", "ldJ", "stR", "trA", "scat"))
        sb = lambda name, shape, dt: st.enter_context(nc.sbuf_tensor(name, shape, dt))
        ps = lambda name, shape: st.enter_context(nc.psum_tensor(name, shape, F32))

        dest_acc = sb("dest_acc", [128, NTALL], F32)
        destu = sb("destu", [128, NTALL], U32)
        cnt_sb = sb("cnt_sb", [8, NEXP], F32)
        l8_sb = sb("l8_sb", [8, 8], F32)
        bases_sb = sb("bases_sb", [8, NEXP], F32)
        mybase = sb("mybase_sb", [128, 1], F32)
        ones_r = sb("ones_r", [1, 128], F32)
        w1_sb = sb("w1_sb", [128, DCH, HID], BF16)
        w2_sb = sb("w2_sb", [128, HID // 128, D], BF16)
        wstg = [sb("wstg%d" % i, [128, HID // 4], F32) for i in range(2)]
        b1col = sb("b1col", [128, EXP_PC * (HID // 128)], F32)
        b2b = sb("b2b", [128, D], F32)
        b2row = sb("b2row", [1, EXP_PC * D], F32)
        buf_fm = sb("buf_fm", [128, DCH, 2048], BF16)
        h_bf = [sb("h_bf%d" % i, [128, 256], BF16) for i in range(2)]
        y_t = sb("y_t", [128, D], BF16)

        pH = [ps("pH%d" % i, [128, 512]) for i in range(2)]
        pY = [ps("pY%d" % i, [128, 512]) for i in range(4)]
        pR = ps("pR2", [128, 512])

        NHCH = HID // 128

        p.dma(sy, "ldA", cnt_sb[:], cnt_in)
        p.dma(sy, "ldA", l8_sb[:], l8_in)
        p.dma(sy, "ldA", mybase[:], mybase_in)
        for e in range(EXP_PC):
            p.dma(sy, "ldA", b1col[:, e * NHCH:(e + 1) * NHCH], b1_in[e, :, :])
            p.dma(sy, "ldA", b2row[0:1, e * D:(e + 1) * D], b2_in[e, :, :])
        t_small = p.tok("ldA")
        p.wait(ve, "ldA", t_small)
        ve.memset(ones_r[:], 1.0)
        ve.memset(y_t[:], 0.0)
        cve0 = p.inc("ve", ve.drain())

        p.wait(sy, "ve", cve0)
        for i in range(SLOTS // 128):
            p.dma(sy, "stR", buf_dram[i * 128:(i + 1) * 128, :], y_t[:])
        t_zero = p.tok("stR")

        p.wait(te, "ldA", t_small)
        p.wait(te, "ve", cve0)
        cte = p.inc("te", te.matmul(pR[0:8, 0:NEXP], l8_sb[:, :], cnt_sb[:, :],
                                    start=True, stop=True))
        p.wait(ac, "te", cte)
        ac.copy(bases_sb[:, :], pR[0:8, 0:NEXP])
        cac_b = p.inc("ac", ac.drain())

        def stage_weights(e, gate_te):
            chunks = []
            for dj in range(DCH):
                for qu in range(4):
                    chunks.append((w1_in[e, dj * 128:(dj + 1) * 128,
                                         qu * 1024:(qu + 1) * 1024],
                                   w1_sb[:, dj, qu * 1024:(qu + 1) * 1024],
                                   1024))
            for hj in range(NHCH):
                chunks.append((w2_in[e, hj * 128:(hj + 1) * 128, :],
                               w2_sb[:, hj, :], D))
            n = len(chunks)
            qs = ["ldA", "ldB"]
            ld_tok = {}

            def emit_load(i):
                src, dst, w = chunks[i]
                ld_tok[i] = p.dma(ac, qs[i % 2], wstg[i % 2][:, 0:w], src)

            last = None
            emit_load(0)
            for i in range(n):
                if i + 1 < n:
                    emit_load(i + 1)
                src, dst, w = chunks[i]
                if i == 0 and gate_te is not None:
                    p.wait(ac, "te", gate_te)
                p.wait(ac, qs[i % 2], ld_tok[i])
                ac.copy(dst, wstg[i % 2][:, 0:w])
                last = p.inc("ac", ac.drain())
            return last

        tr2 = ExitStack()
        sbt = lambda name, shape, dt: tr2.enter_context(nc.sbuf_tensor(name, shape, dt))
        idx_all = sbt("idx_all_sb", [128, NTALL], F32)
        lpos_all = sbt("lpos_all_sb", [128, NTALL], F32)
        bsel = sbt("bsel", [128, NTALL], F32)
        scrM = sbt("scrM", [128, NEXP * NT], F32)
        basb_all = sbt("basb_all", [128, NCORE * NEXP], F32)
        braw = sbt("braw", [1, NCORE * NEXP], F32)
        scr1 = scrM[:, 0:NTALL]
        scr2 = scrM[:, NTALL:2 * NTALL]

        p.dma(sy, "ldG", idx_all[:], idx_in)
        p.dma(sy, "ldG", lpos_all[:], lpos_in)
        p.wait(sy, "ac", cac_b)
        for r in range(NCORE):
            p.dma(sy, "ldG", braw[0:1, r * NEXP:(r + 1) * NEXP],
                  bases_sb[r:r + 1, 0:NEXP])
        tb = p.tok("ldG")
        p.wait(te, "ldG", tb)
        cte = p.inc("te", te.matmul(pR[:, 0:NCORE * NEXP], ones_r[:, :], braw[:, :],
                                    start=True, stop=True))
        p.wait(ac, "te", cte)
        ac.copy(basb_all[:, :], pR[:, 0:NCORE * NEXP])
        cac_bb = p.inc("ac", ac.drain())
        p.wait(ve, "ac", cac_bb)
        p.wait(ve, "ldG", tb)
        p.wait(te, "ac", cac_bb)
        c_wready = stage_weights(0, None)
        for r in range(NCORE):
            blk = slice(r * NT, r * NT + NT)
            for e in range(NEXP):
                ve.tensor_scalar(scrM[:, e * NT:(e + 1) * NT], idx_all[:, blk],
                                 float(e), None, op0=ALU.is_equal)
            ve.drain()
            for e in range(NEXP):
                ve.tensor_scalar(scrM[:, e * NT:(e + 1) * NT],
                                 scrM[:, e * NT:(e + 1) * NT],
                                 basb_all[:, r * NEXP + e:r * NEXP + e + 1],
                                 None, op0=ALU.mult)
            ve.drain()
            for step in (8, 4, 2, 1):
                for i in range(step):
                    ve.tensor_add(scrM[:, i * NT:(i + 1) * NT],
                                  scrM[:, i * NT:(i + 1) * NT],
                                  scrM[:, (i + step) * NT:(i + step + 1) * NT])
                ve.drain()
            ve.tensor_copy(bsel[:, blk], scrM[:, 0:NT])
        ve.drain()
        ve.tensor_add(bsel[:], bsel[:], lpos_all[:])
        ve.drain()
        ve.tensor_scalar(scr1[:], bsel[:], float(CAP), None, op0=ALU.is_lt)
        ve.drain()
        ve.scalar_tensor_tensor(scr2[:], idx_all[:], float(CAP), bsel[:],
                                op0=ALU.mult, op1=ALU.add)
        ve.drain()
        ve.tensor_mul(scr2[:], scr2[:], scr1[:])
        ve.drain()
        ve.tensor_scalar(scr1[:], scr1[:], -BIGF, BIGF, op0=ALU.mult, op1=ALU.add)
        ve.drain()
        ve.tensor_add(dest_acc[:], scr2[:], scr1[:])
        ve.drain()
        ve.tensor_scalar(scr2[:], dest_acc[:], mybase[:, 0:1], None, op0=ALU.subtract)
        ve.drain()
        ve.tensor_scalar(scr1[:], scr2[:], 0.0, None, op0=ALU.is_ge)
        ve.drain()
        ve.tensor_mul(scr2[:], scr2[:], scr1[:])
        ve.drain()
        ve.tensor_scalar(scr1[:], scr1[:], -BIGF, BIGF, op0=ALU.mult, op1=ALU.add)
        ve.drain()
        ve.tensor_add(scr2[:], scr2[:], scr1[:])
        ve.drain()
        ve.tensor_copy(destu[:], scr2[:])
        cve_dest = p.inc("ve", ve.drain())
        p.wait(sy, "ve", cve_dest)
        p.dma(sy, "stR", dest_out[:, :], dest_acc[:, :])
        tr2.close()
        row_t = [sb("row_t%d" % i, [128, D], BF16) for i in range(4)]

        p.wait(gp, "ve", cve_dest)
        p.wait(gp, "stR", t_zero)
        bc_reg = gp.to_reg(SLOTS - 1)
        ev_sc = {}
        ldq = ["ldG", "ldH", "ldI", "ldJ"]
        for g in range(NTALL):
            par = g % 4
            if g >= 4:
                p.wait(sy, "scat", ev_sc[g - 4])
            tl = p.dma(sy, ldq[par], row_t[par][:], rows_in[g * 128:(g + 1) * 128, :])
            p.wait(gp, ldq[par], tl)
            ev_sc[g] = p.inc("scat", gp.indirect_dma_start(
                out=buf_dram,
                out_offset=IndirectOffsetOnAxis(destu[:, g:g + 1], axis=0),
                in_=row_t[par][:], in_offset=None,
                bounds_check=bc_reg, oob_is_err=False), 16)
        t_scat = p.tok("scat")

        for e in range(EXP_PC):
            if e > 0:
                c_wready = stage_weights(e, p.tok("te"))
            p.wait(te, "ldA", t_small)
            p.wait(te, "ve", p.tok("ve"))
            cte = p.inc("te", te.matmul(pR[:, 0:512], ones_r[:, :],
                                        b2row[0:1, e * D:e * D + 512],
                                        start=True, stop=True))
            p.wait(ac, "te", cte)
            ac.copy(b2b[:, 0:512], pR[:, 0:512])
            cacb = p.inc("ac", ac.drain())
            p.wait(te, "ac", cacb)
            cte = p.inc("te", te.matmul(pR[:, 0:512], ones_r[:, :],
                                        b2row[0:1, e * D + 512:(e + 1) * D],
                                        start=True, stop=True))
            p.wait(ac, "te", cte)
            ac.copy(b2b[:, 512:1024], pR[:, 0:512])
            cacb2 = p.inc("ac", ac.drain())
            c_wall = max(c_wready, cacb2)

            for stile in range(CAP // 2048):
                s0 = e * CAP + stile * 2048
                p.wait(sy, "scat", t_scat)
                p.wait(sy, "te", p.tok("te"))
                for dj in range(DCH):
                    p.dma(sy, "trA", buf_fm[:, dj, :],
                          buf_dram[s0:s0 + 2048, dj * 128:(dj + 1) * 128],
                          transpose=True)
                t_fm = p.tok("trA")
                p.wait(te, "trA", t_fm)
                p.wait(te, "ac", c_wall)
                for blk in range(8):
                    b0 = blk * 256
                    evh = {}
                    for hj in range(NHCH):
                        par = hj % 2
                        if hj >= 2:
                            p.wait(te, "ac", evh[hj - 2])
                        for dj in range(DCH):
                            mi = te.matmul(pH[par][:, 0:256],
                                           w1_sb[:, dj, hj * 128:(hj + 1) * 128],
                                           buf_fm[:, dj, b0:b0 + 256],
                                           start=(dj == 0), stop=(dj == DCH - 1))
                        cte = p.inc("te", mi)
                        p.wait(ac, "te", cte)
                        ac.activation(h_bf[par][:, :], pH[par][:, 0:256], AF.Relu,
                                      bias=b1col[:, e * NHCH + hj:e * NHCH + hj + 1])
                        cach = p.inc("ac", ac.drain())
                        evh[hj] = cach
                        p.wait(te, "ac", cach)
                        for s128 in range(2):
                            for half in range(2):
                                mi = te.matmul(
                                    pY[s128 * 2 + half][:, :],
                                    h_bf[par][:, s128 * 128:(s128 + 1) * 128],
                                    w2_sb[:, hj, half * 512:(half + 1) * 512],
                                    start=(hj == 0), stop=(hj == NHCH - 1))
                        if hj == NHCH - 1:
                            cte_y = p.inc("te", mi)
                    p.wait(ve, "te", cte_y)
                    p.wait(ve, "ac", cacb2)
                    for s128 in range(2):
                        r0 = s0 + b0 + s128 * 128
                        p.wait(ve, "stR", p.tok("stR"))
                        ve.tensor_add(y_t[:, 0:512], pY[s128 * 2][:, :], b2b[:, 0:512])
                        ve.tensor_add(y_t[:, 512:1024], pY[s128 * 2 + 1][:, :],
                                      b2b[:, 512:1024])
                        cve = p.inc("ve", ve.drain())
                        p.wait(sy, "ve", cve)
                        p.dma(sy, "stR", y_out[r0:r0 + 128, :], y_t[:])
                    p.wait(te, "ve", p.tok("ve"))
        p.wait(sy, "stR")
    return nc


def _emit_gather_add(nc, p, sb_alloc, y_all_ap, dest_ap, gate_ap, s_in_ap, s_out_dram,
                     bc_tokall):
    te, ve, ac, gp, sy = nc.tensor, nc.vector, nc.scalar, nc.gpsimd, nc.sync
    dest_sb = sb_alloc("ga_dest", [128, NT], F32)
    gate_sb = sb_alloc("ga_gate", [128, NT], F32)
    destu = sb_alloc("ga_destu", [128, NT], U32)
    yrow = [sb_alloc("ga_yrow%d" % i, [128, D], BF16) for i in range(2)]
    s_t = sb_alloc("ga_s", [128, D], F32)
    o_t = sb_alloc("ga_o", [128, D], F32)
    t1 = p.dma(sy, "ldA", dest_sb[:], dest_ap)
    t2 = p.dma(sy, "ldA", gate_sb[:], gate_ap)
    p.wait(ve, "ldA", max(t1, t2))
    ve.tensor_copy(destu[:], dest_sb[:])
    ve.memset(yrow[0][:], 0.0)
    ve.memset(yrow[1][:], 0.0)
    cve0 = p.inc("ve", ve.drain())
    p.wait(gp, "ve", cve0)
    ev_use = {}
    for t in range(NT):
        par = t % 2
        if t >= 2:
            p.wait(gp, "ve", ev_use[t - 2])
        p.inc("scat", gp.indirect_dma_start(
            out=yrow[par][:], out_offset=None,
            in_=y_all_ap, in_offset=IndirectOffsetOnAxis(destu[:, t:t + 1], axis=0),
            bounds_check=bc_tokall, oob_is_err=False), 16)
        ts = p.dma(sy, "ldB", s_t[:], s_in_ap[t * 128:(t + 1) * 128, :])
        p.wait(ve, "scat", p.tok("scat"))
        p.wait(ve, "ldB", ts)
        p.wait(ve, "stR", p.tok("stR"))
        ve.scalar_tensor_tensor(o_t[:], yrow[par][:], gate_sb[:, t:t + 1], s_t[:],
                                op0=ALU.mult, op1=ALU.add)
        cve = p.inc("ve", ve.drain())
        ve.memset(yrow[par][:], 0.0)
        ev_use[t] = p.inc("ve", ve.drain())
        p.wait(sy, "ve", cve)
        p.dma(sy, "stR", s_out_dram[t * 128:(t + 1) * 128, :], o_t[:])
    return p.tok("stR")


def build_L3():
    nc = bass.Bass("TRN2", target_bir_lowering=False, debug=False,
                   detect_race_conditions=False)
    st = ExitStack()
    s1_in = nc.dram_tensor("s1", [TOK, D], F32, kind="ExternalInput").ap()
    yall_in = nc.dram_tensor("y_all", [TOKALL, D], BF16, kind="ExternalInput").ap()
    destm_in = nc.dram_tensor("dest_mine", [128, NT], F32, kind="ExternalInput").ap()
    gatem_in = nc.dram_tensor("gate_mine", [128, NT], F32, kind="ExternalInput").ap()
    inw_in = nc.dram_tensor("in_w", [D, 3 * CIN], F32, kind="ExternalInput").ap()
    inb_in = nc.dram_tensor("in_b", [128, 3 * CIN // 128], F32, kind="ExternalInput").ap()
    convw_in = nc.dram_tensor("conv_w", [128, CIN // 128, 3], F32, kind="ExternalInput").ap()
    convb_in = nc.dram_tensor("conv_b", [128, CIN // 128], F32, kind="ExternalInput").ap()
    alog_in = nc.dram_tensor("a_log", [128, CIN // 128 * DST], F32, kind="ExternalInput").ap()
    bp_in = nc.dram_tensor("b_p", [128, CIN // 128 * DST], F32, kind="ExternalInput").ap()
    cp_in = nc.dram_tensor("c_p", [128, CIN // 128 * DST], F32, kind="ExternalInput").ap()
    dp_in = nc.dram_tensor("d_p", [128, CIN // 128], F32, kind="ExternalInput").ap()
    outw_in = nc.dram_tensor("out_w", [CIN, D], F32, kind="ExternalInput").ap()
    outb_in = nc.dram_tensor("out_b", [1, D], F32, kind="ExternalInput").ap()
    gam_in = nc.dram_tensor("gam", [1, D], F32, kind="ExternalInput").ap()
    bet_in = nc.dram_tensor("bet", [1, D], F32, kind="ExternalInput").ap()
    wg_in = nc.dram_tensor("wg", [D, NEXP], F32, kind="ExternalInput").ap()
    bg_in = nc.dram_tensor("bg", [1, NEXP], F32, kind="ExternalInput").ap()

    s3_out = nc.dram_tensor("s3", [TOK, D], F32, kind="ExternalOutput").ap()
    rows_out = nc.dram_tensor("rows2", [TOK, D], BF16, kind="ExternalOutput").ap()
    idx_out = nc.dram_tensor("idx2", [128, NT], F32, kind="ExternalOutput").ap()
    gate_out = nc.dram_tensor("gate2", [128, NT], F32, kind="ExternalOutput").ap()
    lpos_out = nc.dram_tensor("lpos2", [128, NT], F32, kind="ExternalOutput").ap()
    cnt_out = nc.dram_tensor("cnt2", [1, NEXP], F32, kind="ExternalOutput").ap()

    s2_dram = nc.dram_tensor("s2_dram", [TOK, D], F32, kind="Internal").ap()
    h_rows = nc.dram_tensor("h_rows", [TOK, D], BF16, kind="Internal").ap()
    uvg_fm = nc.dram_tensor("uvg_fm", [3 * CIN, TOK], BF16, kind="Internal").ap()
    yg_fm = nc.dram_tensor("yg_fm", [CIN, TOK], BF16, kind="Internal").ap()
    lo_rows = nc.dram_tensor("lo_rows2", [TOK, D], BF16, kind="Internal").ap()

    te, ve, ac, gp, sy = nc.tensor, nc.vector, nc.scalar, nc.gpsimd, nc.sync
    NG = CIN // 128
    NIN = 3 * CIN // 128

    def barrier(p):
        for eng in (ve, ac, te, gp, sy):
            for key in p.sem:
                p.wait(eng, key)

    with st:
        p = P(nc, st, ("gp", "te", "ve", "ac", "ldA", "ldB", "ldG", "ldH", "ldS",
                       "stR", "stS", "stH", "trA", "scat"))
        sb = lambda name, shape, dt: st.enter_context(nc.sbuf_tensor(name, shape, dt))
        ps = lambda name, shape: st.enter_context(nc.psum_tensor(name, shape, F32))

        gamb = sb("gamb", [128, D], F32)
        betb = sb("betb", [128, D], F32)
        outbb = sb("outbb", [128, D], F32)
        wg_hi = sb("wg_hi3", [128, DCH, NEXP], BF16)
        wg_lo = sb("wg_lo3", [128, DCH, NEXP], BF16)
        bgb = sb("bgb3", [128, NEXP], F32)
        ones_r = sb("ones_r3", [1, 128], F32)
        ones_k = sb("ones_k3", [128, 1], F32)
        trilbf = sb("trilbf3", [128, 128], BF16)
        eye_bf = sb("eye_bf", [128, 128], BF16)
        iota_e = sb("iota_e3", [128, NEXP], F32)
        iota_me = sb("iota_me3", [128, NEXP], F32)
        run_tot = sb("run_tot3", [1, NEXP], F32)
        inb_sb = sb("inb_sb", [128, NIN], F32)
        convw_sb = sb("convw_sb", [128, NG, 3], F32)
        convb_sb = sb("convb_sb", [128, NG], F32)
        dp_sb = sb("dp_sb", [128, NG], F32)
        ktaps = sb("ktaps", [128, TAPS, NG], F32)
        mcol = sb("mcol3", [128, 1], F32)
        eps_col = sb("eps_col3", [128, 1], F32)
        sqs = sb("sqs3", [128, 1], F32)
        vcol = sb("vcol3", [128, 1], F32)
        idx_acc = sb("idx_acc3", [128, NT], F32)
        gate_acc = sb("gate_acc3", [128, NT], F32)
        lpos_acc = sb("lpos_acc3", [128, NT], F32)
        logit_t = sb("logit_t3", [128, 32], F32)
        mx_t = sb("mx_t3", [128, 1], F32)
        ssum_t = sb("ssum_t3", [128, 1], F32)
        gate_col = sb("gate_col3", [128, 1], F32)
        eq_t = sb("eq_t3", [128, NEXP], F32)
        cand_t = sb("cand_t3", [128, 32], F32)
        idx_col = sb("idx_col3", [128, 1], F32)
        oh_t = sb("oh_t3", [128, NEXP], BF16)
        ohf_t = sb("ohf_t3", [128, NEXP], F32)
        gsel_t = sb("gsel_t3", [128, 32], F32)
        lpos_col = sb("lpos_col3", [128, 1], F32)
        runb = sb("runb3", [128, NEXP], F32)
        s_t = sb("s_t3", [128, D], F32)
        tmp_t = sb("tmp_t3", [128, D], F32)
        tmp2_t = sb("tmp2_t3", [128, D], F32)
        hi32_t = sb("hi32_t3", [128, D], F32)
        lnbf_t = sb("lnbf_t3", [128, D], BF16)
        lo_t = sb("lo_t3", [128, D], BF16)

        pA = ps("pA3", [128, 512])
        pB = ps("pB3", [128, 512])
        pC = ps("pC3", [128, 512])
        pD = ps("pD3", [128, 512])
        pR = ps("pR3", [128, 512])
        pG3 = ps("pG3", [128, 512])

        bc_reg = gp.to_reg(TOKALL - 1)
        ga = ExitStack()
        sba = lambda name, shape, dt: ga.enter_context(nc.sbuf_tensor(name, shape, dt))
        _emit_gather_add(nc, p, sba, yall_in, destm_in, gatem_in, s1_in,
                         s2_dram, bc_reg)
        barrier(p)
        ga.close()

        gp.iota(iota_e[:], pattern=[[1, NEXP]], base=0, channel_multiplier=0,
                allow_small_or_imprecise_dtypes=True)
        gp.iota(tmp_t[:, 0:128], pattern=[[1, 128]], base=0, channel_multiplier=0,
                allow_small_or_imprecise_dtypes=True)
        c_iota = p.inc("gp", gp.iota(mcol[:], pattern=[[0, 1]], base=0,
                                     channel_multiplier=1,
                                     allow_small_or_imprecise_dtypes=True))
        p.wait(ve, "gp", c_iota)
        ve.memset(ones_r[:], 1.0)
        ve.memset(ones_k[:], 1.0)
        ve.memset(run_tot[:], 0.0)
        ve.memset(eps_col[:], EPS)
        ve.memset(logit_t[:, 16:32], -1.0e30)
        ve.memset(cand_t[:, 16:32], 1.0e30)
        ve.memset(gsel_t[:, 16:32], 0.0)
        ve.tensor_scalar_add(iota_me[:], iota_e[:], -65536.0)
        ve.drain()
        ve.tensor_scalar(tmp2_t[:, 0:128], tmp_t[:, 0:128], mcol[:, 0:1], None,
                         op0=ALU.is_gt)
        ve.drain()
        ve.tensor_copy(trilbf[:], tmp2_t[:, 0:128])
        ve.drain()
        ve.tensor_scalar(tmp2_t[:, 0:128], tmp_t[:, 0:128], mcol[:, 0:1], None,
                         op0=ALU.is_equal)
        ve.drain()
        ve.tensor_copy(eye_bf[:], tmp2_t[:, 0:128])
        c_const = p.inc("ve", ve.drain())

        p.dma(sy, "ldA", inb_sb[:], inb_in)
        p.dma(sy, "ldA", convw_sb[:], convw_in)
        p.dma(sy, "ldA", convb_sb[:], convb_in)
        p.dma(sy, "ldA", dp_sb[:], dp_in)
        t_par = p.tok("ldA")
        grow = sb("grow", [1, D], F32)
        for src_ap, dstb in ((gam_in, gamb), (bet_in, betb), (outb_in, outbb)):
            tg = p.dma(sy, "ldB", grow[:], src_ap)
            p.wait(te, "ldB", tg)
            p.wait(te, "ve", c_const)
            for half in range(2):
                cte = p.inc("te", te.matmul(pR[:, :], ones_r[:, :],
                                            grow[:, half * 512:(half + 1) * 512],
                                            start=True, stop=True))
                p.wait(ac, "te", cte)
                ac.copy(dstb[:, half * 512:(half + 1) * 512], pR[:, :])
                cac = p.inc("ac", ac.drain())
                p.wait(te, "ac", cac)
            p.wait(sy, "ac", p.tok("ac"))
        bg_row = sb("bg_row3", [1, NEXP], F32)
        tg = p.dma(sy, "ldB", bg_row[:], bg_in)
        p.wait(te, "ldB", tg)
        cte = p.inc("te", te.matmul(pR[:, 0:NEXP], ones_r[:, :], bg_row[:, :],
                                    start=True, stop=True))
        p.wait(ac, "te", cte)
        ac.copy(bgb[:, :], pR[:, 0:NEXP])
        c_bgb = p.inc("ac", ac.drain())
        p.wait(te, "ac", c_bgb)
        wg32 = sb("wg32_3", [128, DCH, NEXP], F32)
        wg_t32 = sb("wg_t32_3", [128, DCH, NEXP], F32)
        for j in range(DCH):
            p.dma(sy, "ldB", wg32[:, j, :], wg_in[j * 128:(j + 1) * 128, :])
        t_wg = p.tok("ldB")
        p.wait(ve, "ldB", t_wg)
        ve.tensor_copy(wg_hi[:, :, :], wg32[:, :, :])
        ve.drain()
        ve.tensor_copy(wg_t32[:, :, :], wg_hi[:, :, :])
        ve.drain()
        ve.tensor_sub(wg_t32[:, :, :], wg32[:, :, :], wg_t32[:, :, :])
        ve.drain()
        c_wg = p.inc("ve", ve.tensor_copy(wg_lo[:, :, :], wg_t32[:, :, :]))

        prow = sb("prow", [128, NG * DST], F32)
        rdec = sb("rdec", [128, NG * DST], F32)
        tB = p.dma(sy, "ldB", prow[:], bp_in)
        tAg = p.dma(sy, "ldB", rdec[:], alog_in)
        p.wait(ac, "ldB", tAg)
        ac.activation(rdec[:], rdec[:], AF.Exp)
        ac.drain()
        ac.activation(rdec[:], rdec[:], AF.Exp, scale=-1.0)
        cac = p.inc("ac", ac.drain())
        tC = p.dma(sy, "ldB", tmp_t[:, 0:NG * DST], cp_in)
        p.wait(ve, "ldB", tC)
        p.wait(ve, "ac", cac)
        ve.tensor_mul(prow[:], prow[:], tmp_t[:, 0:NG * DST])
        ve.drain()
        pv = prow.rearrange("p (g j) -> p g j", g=NG)
        for dl in range(TAPS):
            ve.tensor_reduce(ktaps[:, dl, :], pv, axis=AX, op=ALU.add)
            ve.drain()
            if dl + 1 < TAPS:
                ve.tensor_mul(prow[:], prow[:], rdec[:])
                ve.drain()
        p.wait(ve, "ldA", t_par)
        ve.tensor_add(ktaps[:, 0, :], ktaps[:, 0, :], dp_sb[:, :])
        c_taps = p.inc("ve", ve.drain())

        phA = ExitStack()
        sbA = lambda name, shape, dt: phA.enter_context(nc.sbuf_tensor(name, shape, dt))
        h_fm = sbA("h_fm3", [128, DCH, 2048], BF16)
        inw_sb = sbA("inw_sb", [128, DCH, 3 * CIN // 2], BF16)
        uvg_t = [sbA("uvg_t%d" % i, [128, 512], BF16) for i in range(4)]
        s_ln = [sbA("s_ln%d" % i, [128, D], F32) for i in range(2)]
        ln_bf = [sbA("ln_bf%d" % i, [128, D], BF16) for i in range(2)]
        QT = 2048
        ev_te_q = p.tok("te")
        ev_ac_u = {}
        ev_st_u = {}
        nU = 0
        ld_ln = {}
        st_ln = {}

        def ac_load_ln(gti):
            if gti >= TOK // 128:
                return
            r = gti * 128
            ld_ln[gti] = p.dma(ac, "ldS", s_ln[gti % 2][:],
                               s2_dram[r:r + 128, :])

        ac_load_ln(0)
        for qi in range(TOK // QT):
            q0 = qi * QT
            for tt in range(QT // 128):
                r0 = q0 + tt * 128
                gti = r0 // 128
                sl = s_ln[gti % 2]
                lb = ln_bf[gti % 2]
                p.wait(ve, "ldS", ld_ln[gti])
                ve.reduce_sum(sqs[:], sl[:], axis=AX, negate=True)
                ve.drain()
                ve.tensor_scalar_mul(mcol[:], sqs[:], 1.0 / D)
                ve.drain()
                cve = p.inc("ve", ve.tensor_scalar_add(tmp_t[:], sl[:], mcol[:, 0:1]))
                p.wait(ac, "ve", cve)
                ac_load_ln(gti + 1)
                ac.activation(hi32_t[:], tmp_t[:], AF.Square, accum_out=sqs[:, 0:1])
                ac.drain()
                ac.activation(vcol[:], sqs[:, 0:1], AF.Ln, scale=1.0 / D,
                              bias=eps_col[:, 0:1])
                ac.drain()
                ac.activation(vcol[:], vcol[:], AF.Exp, scale=-0.5)
                cac = p.inc("ac", ac.drain())
                p.wait(ve, "ac", cac)
                ve.tensor_scalar(tmp2_t[:], tmp_t[:], vcol[:, 0:1], None, op0=ALU.mult)
                ve.drain()
                ve.reduce_sum(sqs[:], tmp2_t[:], axis=AX, negate=True)
                ve.drain()
                ve.tensor_scalar_mul(mcol[:], sqs[:], 1.0 / D)
                ve.drain()
                cve = p.inc("ve", ve.tensor_scalar_add(tmp_t[:], tmp2_t[:], mcol[:, 0:1]))
                p.wait(ac, "ve", cve)
                ac.activation(hi32_t[:], tmp_t[:], AF.Square, accum_out=sqs[:, 0:1])
                ac.drain()
                ac.activation(vcol[:], sqs[:, 0:1], AF.Ln, scale=1.0 / D,
                              bias=eps_col[:, 0:1])
                ac.drain()
                ac.activation(vcol[:], vcol[:], AF.Exp, scale=-0.5)
                cac = p.inc("ac", ac.drain())
                p.wait(ve, "ac", cac)
                ve.tensor_scalar(tmp_t[:], tmp_t[:], vcol[:, 0:1], None, op0=ALU.mult)
                ve.drain()
                ve.tensor_mul(tmp_t[:], tmp_t[:], gamb[:])
                ve.drain()
                ve.tensor_add(tmp_t[:], tmp_t[:], betb[:])
                ve.drain()
                p.wait(ve, "stH", st_ln.get(gti - 2, 0))
                ve.tensor_copy(lb[:], tmp_t[:])
                cve2 = p.inc("ve", ve.drain())
                p.wait(ac, "ve", cve2)
                st_ln[gti] = p.dma(ac, "stH", h_rows[r0:r0 + 128, :], lb[:])
            t_hq = p.tok("stH")
            p.wait(sy, "stH", t_hq)
            p.wait(sy, "te", ev_te_q)
            for j in range(DCH):
                p.dma(sy, "trA", h_fm[:, j, :],
                      h_rows[q0:q0 + QT, j * 128:(j + 1) * 128], transpose=True)
            t_fm = p.tok("trA")
            p.wait(te, "trA", t_fm)
            for half2 in range(2):
                ch0 = half2 * (NIN // 2)
                p.wait(gp, "te", p.tok("te"))
                for j in range(DCH):
                    p.dma(gp, "ldG", inw_sb[:, j, :],
                          inw_in[j * 128:(j + 1) * 128,
                                 ch0 * 128:(ch0 + NIN // 2) * 128])
                t_inw = p.tok("ldG")
                p.wait(te, "ldG", t_inw)
                for cc in range(NIN // 2):
                    cch = ch0 + cc
                    for nt4 in range(QT // 512):
                        pb = pA if (nU % 2 == 0) else pB
                        if nU >= 2:
                            p.wait(te, "ac", ev_ac_u[nU - 2])
                        for j in range(DCH):
                            mi = te.matmul(pb[:, :],
                                           inw_sb[:, j, cc * 128:(cc + 1) * 128],
                                           h_fm[:, j, nt4 * 512:(nt4 + 1) * 512],
                                           start=(j == 0), stop=(j == DCH - 1))
                        cte = p.inc("te", mi)
                        ub = uvg_t[nU % 4]
                        stq = "stR" if nU % 2 == 0 else "stS"
                        p.wait(ac, "te", cte)
                        if nU >= 4:
                            p.wait(ac, ev_st_u[nU - 4][0], ev_st_u[nU - 4][1])
                        ac.activation(ub[:], pb[:, :], AF.Identity,
                                      bias=inb_sb[:, cch:cch + 1])
                        cacu = p.inc("ac", ac.drain())
                        ev_ac_u[nU] = cacu
                        p.wait(sy, "ac", cacu)
                        ev_st_u[nU] = (stq, p.dma(sy, stq, uvg_fm[
                            cch * 128:(cch + 1) * 128,
                            q0 + nt4 * 512:q0 + (nt4 + 1) * 512], ub[:]))
                        nU += 1
            ev_ve_prev = p.tok("ve")
            ev_te_q = p.tok("te")
        barrier(p)
        phA.close()

        phB = ExitStack()
        sbB = lambda name, shape, dt: phB.enter_context(nc.sbuf_tensor(name, shape, dt))
        u_bf = [sbB("u_bf%d" % i, [128, TOK], BF16) for i in range(2)]
        v_bf = [sbB("v_bf%d" % i, [128, TOK], BF16) for i in range(2)]
        g_bf = [sbB("g_bf%d" % i, [128, TOK], BF16) for i in range(2)]
        u_f32 = sbB("u_f32", [128, TOK], F32)
        ua_bf = sbB("ua_bf", [128, TOK], BF16)
        diag_sb = sbB("diag_sb", [128, TAPS, 128], BF16)
        sgt = sbB("sgt", [128, 512], F32)
        svt = sbB("svt", [128, 512], F32)
        ygt = sbB("ygt", [128, 512], BF16)
        yf32 = sbB("yf32", [128, 512], F32)
        ev_gate_prev = 0
        ld_tok = {}
        cve_conv = {}
        cac_gate = {}
        cac_silu = {}
        cte_taps = {}

        def issue_uvg_loads(g):
            par = g % 2
            if g - 2 in cve_conv:
                p.wait(sy, "ve", cve_conv[g - 2])
                p.wait(sy, "ac", cac_gate[g - 2])
            tu = p.dma(sy, "ldA", u_bf[par][:], uvg_fm[g * 128:(g + 1) * 128, :])
            tv = p.dma(sy, "ldB", v_bf[par][:],
                       uvg_fm[CIN + g * 128:CIN + (g + 1) * 128, :])
            tg2 = p.dma(sy, "ldH", g_bf[par][:],
                        uvg_fm[2 * CIN + g * 128:2 * CIN + (g + 1) * 128, :])
            ld_tok[g] = (tu, tv, tg2)

        issue_uvg_loads(0)
        for g in range(NG):
            par = g % 2
            if g + 1 < NG:
                issue_uvg_loads(g + 1)
            tu, tv, tg2 = ld_tok[g]
            u_b, v_b, g_b = u_bf[par], v_bf[par], g_bf[par]
            p.wait(ve, "ldA", tu)
            p.wait(ve, "ac", cac_silu.get(g - 1, 0))
            for s in range(BLOC):
                t0 = s * SEQ
                ve.tensor_scalar(u_f32[:, t0:t0 + SEQ], u_b[:, t0:t0 + SEQ],
                                 convw_sb[:, g, 1:2], None, op0=ALU.mult)
                ve.drain()
                ve.scalar_tensor_tensor(u_f32[:, t0 + 1:t0 + SEQ],
                                        u_b[:, t0:t0 + SEQ - 1],
                                        convw_sb[:, g, 0:1],
                                        u_f32[:, t0 + 1:t0 + SEQ],
                                        op0=ALU.mult, op1=ALU.add)
                ve.drain()
                ve.scalar_tensor_tensor(u_f32[:, t0:t0 + SEQ - 1],
                                        u_b[:, t0 + 1:t0 + SEQ],
                                        convw_sb[:, g, 2:3],
                                        u_f32[:, t0:t0 + SEQ - 1],
                                        op0=ALU.mult, op1=ALU.add)
                ve.drain()
            cve_conv[g] = p.inc("ve", ve.drain())
            p.wait(ac, "ve", cve_conv[g])
            ac.activation(u_f32[:], u_f32[:], AF.Silu, bias=convb_sb[:, g:g + 1])
            cac_silu[g] = p.inc("ac", ac.drain())
            p.wait(ve, "ac", cac_silu[g])
            p.wait(ve, "te", cte_taps.get(g - 1, 0))
            p.inc("ve", ve.tensor_copy(ua_bf[:], u_f32[:]))
            for dl in range(TAPS):
                ve.tensor_scalar(diag_sb[:, dl, :], eye_bf[:], ktaps[:, dl, g:g + 1],
                                 None, op0=ALU.mult)
            cve_d = p.inc("ve", ve.drain())
            p.wait(te, "ve", cve_d)
            for s in range(BLOC):
                for nt4 in range(SEQ // 512):
                    t0 = s * SEQ + nt4 * 512
                    tloc = nt4 * 512
                    pb = pA if nt4 % 2 == 0 else pB
                    first = True
                    for dl in range(TAPS):
                        lo = max(0, dl - tloc)
                        if lo >= 512:
                            continue
                        src0 = t0 + lo - dl
                        n = 512 - lo
                        mi = te.matmul(pb[:, lo:512], diag_sb[:, dl, :],
                                       ua_bf[:, src0:src0 + n],
                                       start=first, stop=(dl == TAPS - 1))
                        first = False
                    cte = p.inc("te", mi)
                    p.wait(ac, "ldB", tv)
                    p.wait(ac, "ldH", tg2)
                    p.wait(ac, "te", cte)
                    p.wait(ac, "ve", ev_gate_prev)
                    ac.activation(sgt[:], g_b[:, t0:t0 + 512], AF.Sigmoid)
                    ac.drain()
                    ac.activation(svt[:], v_b[:, t0:t0 + 512], AF.Silu)
                    cac = p.inc("ac", ac.drain())
                    cac_gate[g] = cac
                    p.wait(ve, "ac", cac)
                    p.wait(ve, "te", cte)
                    p.wait(ve, "stR", p.tok("stR"))
                    ve.tensor_mul(yf32[:], pb[:, :], sgt[:])
                    ve.drain()
                    ve.tensor_add(yf32[:], yf32[:], svt[:])
                    ve.drain()
                    ve.tensor_copy(ygt[:], yf32[:])
                    cve = p.inc("ve", ve.drain())
                    ev_gate_prev = cve
                    p.wait(te, "ve", cve)
                    p.wait(sy, "ve", cve)
                    p.dma(sy, "stR", yg_fm[g * 128:(g + 1) * 128, t0:t0 + 512], ygt[:])
            cte_taps[g] = p.tok("te")
        barrier(p)
        phB.close()

        phC = ExitStack()
        sbC = lambda name, shape, dt: phC.enter_context(nc.sbuf_tensor(name, shape, dt))
        outw_sb = sbC("outw_sb", [128, NG, D], BF16)
        for g in range(NG):
            p.dma(gp, "ldG", outw_sb[:, g, :], outw_in[g * 128:(g + 1) * 128, :])
        t_outw = p.tok("ldG")
        yg_sb = [sbC("yg_sb%d" % i, [128, NG, 128], BF16) for i in range(2)]
        hi_fm = sbC("hi_fmC", [128, DCH, 1024], BF16)
        lo_fm = sbC("lo_fmC", [128, DCH, 1024], BF16)
        QT = 1024
        tile_i = 0
        ev_ve_router_prev = p.tok("ve")
        ev_te_qtile = p.tok("te")
        t_yg_tok = {}
        cte_out = {}

        def load_yg(ti):
            if ti >= TOK // 128:
                return
            p.wait(sy, "te", cte_out.get(ti - 2, 0))
            for g in range(NG):
                p.dma(sy, "ldB", yg_sb[ti % 2][:, g, :],
                      yg_fm[g * 128:(g + 1) * 128, ti * 128:(ti + 1) * 128])
            t_yg_tok[ti] = p.tok("ldB")

        load_yg(0)
        for qi in range(TOK // QT):
            q0 = qi * QT
            for tt in range(QT // 128):
                r0 = q0 + tt * 128
                ti = r0 // 128
                load_yg(ti + 1)
                ts = p.dma(sy, "ldA", s_t[:], s2_dram[r0:r0 + 128, :])
                p.wait(te, "ldB", t_yg_tok[ti])
                p.wait(te, "ldG", t_outw)
                p.wait(te, "ve", ev_ve_router_prev)
                for half in range(2):
                    pb = pA if half == 0 else pB
                    for g in range(NG):
                        mi = te.matmul(pb[:, :], yg_sb[ti % 2][:, g, :],
                                       outw_sb[:, g, half * 512:(half + 1) * 512],
                                       start=(g == 0), stop=(g == NG - 1))
                cte = p.inc("te", mi)
                cte_out[ti] = cte
                p.wait(ve, "te", cte)
                p.wait(ve, "ldA", ts)
                p.wait(ve, "stR", p.tok("stR"))
                ve.reduce_sum(sqs[:], s_t[:], axis=AX, negate=True)
                ve.drain()
                ve.tensor_scalar_mul(mcol[:], sqs[:], 1.0 / D)
                ve.drain()
                cve = p.inc("ve", ve.tensor_scalar_add(tmp_t[:], s_t[:], mcol[:, 0:1]))
                p.wait(ac, "ve", cve)
                ac.activation(hi32_t[:], tmp_t[:], AF.Square, accum_out=sqs[:, 0:1])
                ac.drain()
                ac.activation(vcol[:], sqs[:, 0:1], AF.Ln, scale=1.0 / D,
                              bias=eps_col[:, 0:1])
                ac.drain()
                ac.activation(vcol[:], vcol[:], AF.Exp, scale=-0.5)
                cac = p.inc("ac", ac.drain())
                p.wait(ve, "ac", cac)
                ve.tensor_scalar(tmp2_t[:], tmp_t[:], vcol[:, 0:1], None, op0=ALU.mult)
                ve.drain()
                ve.tensor_add(tmp2_t[:], tmp2_t[:], s_t[:])
                ve.drain()
                ve.tensor_add(tmp2_t[:, 0:512], tmp2_t[:, 0:512], pA[:, :])
                ve.drain()
                ve.tensor_add(tmp2_t[:, 512:1024], tmp2_t[:, 512:1024], pB[:, :])
                ve.drain()
                cve = p.inc("ve", ve.tensor_add(s_t[:], tmp2_t[:], outbb[:]))
                p.wait(te, "ve", cve)
                p.wait(sy, "ve", cve)
                p.dma(sy, "stR", s3_out[r0:r0 + 128, :], s_t[:])
                ve.reduce_sum(sqs[:], s_t[:], axis=AX, negate=True)
                ve.drain()
                ve.tensor_scalar_mul(mcol[:], sqs[:], 1.0 / D)
                ve.drain()
                cve = p.inc("ve", ve.tensor_scalar_add(tmp_t[:], s_t[:], mcol[:, 0:1]))
                p.wait(ac, "ve", cve)
                ac.activation(hi32_t[:], tmp_t[:], AF.Square, accum_out=sqs[:, 0:1])
                ac.drain()
                ac.activation(vcol[:], sqs[:, 0:1], AF.Ln, scale=1.0 / D,
                              bias=eps_col[:, 0:1])
                ac.drain()
                ac.activation(vcol[:], vcol[:], AF.Exp, scale=-0.5)
                cac = p.inc("ac", ac.drain())
                p.wait(ve, "ac", cac)
                ve.tensor_scalar(lnbf_t[:], tmp_t[:], vcol[:, 0:1], None, op0=ALU.mult)
                ve.tensor_scalar(tmp_t[:], tmp_t[:], vcol[:, 0:1], None, op0=ALU.mult)
                ve.drain()
                ve.tensor_copy(hi32_t[:], lnbf_t[:])
                ve.drain()
                ve.tensor_sub(tmp_t[:], tmp_t[:], hi32_t[:])
                ve.drain()
                ve.tensor_copy(lo_t[:], tmp_t[:])
                cve = p.inc("ve", ve.drain())
                p.wait(sy, "ve", cve)
                p.dma(sy, "stR", rows_out[r0:r0 + 128, :], lnbf_t[:])
                p.dma(sy, "stR", lo_rows[r0:r0 + 128, :], lo_t[:])
            t_rows_q = p.tok("stR")
            ev_te_outproj = p.tok("te")
            p.wait(sy, "stR", t_rows_q)
            p.wait(sy, "te", ev_te_outproj)
            for j in range(DCH):
                p.dma(sy, "trA", hi_fm[:, j, :],
                      rows_out[q0:q0 + QT, j * 128:(j + 1) * 128], transpose=True)
                p.dma(sy, "trA", lo_fm[:, j, :],
                      lo_rows[q0:q0 + QT, j * 128:(j + 1) * 128], transpose=True)
            t_rfm = p.tok("trA")
            p.wait(te, "trA", t_rfm)
            p.wait(te, "ve", c_wg)
            for tt in range(QT // 128):
                t0 = tt * 128
                p.wait(te, "ve", ev_ve_router_prev)
                for j in range(DCH):
                    te.matmul(pR[:, 0:NEXP], hi_fm[:, j, t0:t0 + 128],
                              wg_hi[:, j, :], start=(j == 0), stop=False)
                for j in range(DCH):
                    te.matmul(pR[:, 0:NEXP], hi_fm[:, j, t0:t0 + 128],
                              wg_lo[:, j, :], start=False, stop=False)
                for j in range(DCH):
                    mi = te.matmul(pR[:, 0:NEXP], lo_fm[:, j, t0:t0 + 128],
                                   wg_hi[:, j, :], start=False, stop=(j == DCH - 1))
                cte = p.inc("te", mi)
                p.wait(ve, "te", cte)
                p.wait(ve, "ac", c_bgb)
                ve.tensor_add(logit_t[:, 0:NEXP], pR[:, 0:NEXP], bgb[:, :])
                cve = p.inc("ve", ve.drain())
                p.wait(te, "ve", cve)
                ve.reduce_max(mx_t[:], logit_t[:, 0:32], axis=AX)
                ve.drain()
                ve.tensor_scalar(eq_t[:], logit_t[:, 0:NEXP], mx_t[:, 0:1],
                                 None, op0=ALU.subtract)
                cve = p.inc("ve", ve.drain())
                p.wait(ac, "ve", cve)
                ac.activation(eq_t[:], eq_t[:], AF.Exp, accum_out=ssum_t[:, 0:1])
                cac = p.inc("ac", ac.drain())
                p.wait(ve, "ac", cac)
                ve.reciprocal(gate_col[:], ssum_t[:])
                ve.drain()
                ve.tensor_scalar(eq_t[:], logit_t[:, 0:NEXP], mx_t[:, 0:1], None,
                                 op0=ALU.is_ge)
                ve.drain()
                ve.tensor_mul(cand_t[:, 0:NEXP], eq_t[:], iota_me[:])
                ve.drain()
                ve.tensor_scalar_add(cand_t[:, 0:NEXP], cand_t[:, 0:NEXP], 65536.0)
                ve.drain()
                ve.tensor_reduce(idx_col[:], cand_t[:, 0:32], axis=AX, op=ALU.min)
                ve.drain()
                ve.tensor_scalar(ohf_t[:], iota_e[:], idx_col[:, 0:1], None,
                                 op0=ALU.is_equal)
                ve.drain()
                ve.tensor_copy(oh_t[:], ohf_t[:])
                cve = p.inc("ve", ve.drain())
                p.wait(te, "ve", cve)
                te.matmul(pR[:, 0:NEXP], trilbf[:, :], oh_t[:, :], start=True, stop=True)
                te.matmul(pC[:, 0:NEXP], ones_r[:, :], run_tot[:, :],
                          start=True, stop=True)
                cte2 = p.inc("te", te.matmul(pG3[0:1, 0:NEXP], ones_k[:, :], ohf_t[:, :],
                                             start=True, stop=True))
                p.wait(ac, "te", cte2)
                ac.copy(runb[:, :], pC[:, 0:NEXP])
                cac = p.inc("ac", ac.drain())
                p.wait(ve, "te", cte2)
                p.wait(ve, "ac", cac)
                ve.tensor_add(gsel_t[:, 0:NEXP], pR[:, 0:NEXP], runb[:, :])
                ve.drain()
                ve.tensor_mul(gsel_t[:, 0:NEXP], gsel_t[:, 0:NEXP], ohf_t[:])
                ve.drain()
                ve.tensor_reduce(lpos_col[:], gsel_t[:, 0:32], axis=AX, op=ALU.add)
                ve.drain()
                ve.tensor_add(run_tot[:, :], run_tot[:, :], pG3[0:1, 0:NEXP])
                ve.drain()
                ve.tensor_copy(idx_acc[:, tile_i:tile_i + 1], idx_col[:])
                ve.tensor_copy(gate_acc[:, tile_i:tile_i + 1], gate_col[:])
                ve.tensor_copy(lpos_acc[:, tile_i:tile_i + 1], lpos_col[:])
                cve = p.inc("ve", ve.drain())
                ev_ve_router_prev = cve
                p.wait(te, "ve", cve)
                ev_te_qtile = p.tok("te")
                tile_i += 1
        cve = p.tok("ve")
        p.wait(sy, "ve", cve)
        p.dma(sy, "stR", idx_out[:, :], idx_acc[:, :])
        p.dma(sy, "stR", gate_out[:, :], gate_acc[:, :])
        p.dma(sy, "stR", lpos_out[:, :], lpos_acc[:, :])
        p.dma(sy, "stR", cnt_out[:, :], run_tot[:, :])
        p.wait(sy, "stR")
        phC.close()
    return nc


def build_L5():
    nc = bass.Bass("TRN2", target_bir_lowering=False, debug=False,
                   detect_race_conditions=False)
    st = ExitStack()
    s3_in = nc.dram_tensor("s3", [TOK, D], F32, kind="ExternalInput").ap()
    yall_in = nc.dram_tensor("y2_all", [TOKALL, D], BF16, kind="ExternalInput").ap()
    destm_in = nc.dram_tensor("dest2_mine", [128, NT], F32, kind="ExternalInput").ap()
    gatem_in = nc.dram_tensor("gate2_mine", [128, NT], F32, kind="ExternalInput").ap()
    out_out = nc.dram_tensor("final", [TOK, D], F32, kind="ExternalOutput").ap()
    gp = nc.gpsimd
    with st:
        p = P(nc, st, ("gp", "te", "ve", "ac", "ldA", "ldB", "stR", "scat"))
        sba = lambda name, shape, dt: st.enter_context(nc.sbuf_tensor(name, shape, dt))
        bc_reg = gp.to_reg(TOKALL - 1)
        _emit_gather_add(nc, p, sba, yall_in, destm_in, gatem_in, s3_in,
                         out_out, bc_reg)
        p.wait(nc.sync, "stR")
    return nc



_CACHE = {}
_DBG = {}


def _get(name, builder):
    if name not in _CACHE:
        _CACHE[name] = builder()
    return _CACHE[name]


def _cores():
    return list(range(NCORE))


def kernel(L0_attn_a_x, L0_attn_a_context, L0_attn_a_Wq, L0_attn_a_Wk,
           L0_attn_a_Wv, L0_attn_a_Wout, L0_attn_a_bout, L0_attn_b_norm_gamma,
           L0_attn_b_norm_beta, L0_attn_b_in_w, L0_attn_b_in_b, L0_attn_b_conv_w,
           L0_attn_b_conv_b, L0_attn_b_A_log, L0_attn_b_B_p, L0_attn_b_C_p,
           L0_attn_b_D_p, L0_attn_b_out_w, L0_attn_b_out_b, L0_ffn_Wg,
           L0_ffn_bg, L0_ffn_W1, L0_ffn_b1, L0_ffn_W2, L0_ffn_b2):
    f32 = lambda a: np.ascontiguousarray(np.asarray(a), dtype=np.float32)
    x = f32(L0_attn_a_x).reshape(BATCH * SEQ, D)
    ctx = f32(L0_attn_a_context).reshape(CTXT, CD)
    wq, wk, wv, wo = map(f32, (L0_attn_a_Wq, L0_attn_a_Wk, L0_attn_a_Wv, L0_attn_a_Wout))
    bout = f32(L0_attn_a_bout).reshape(1, D)
    wg = f32(L0_ffn_Wg)
    bg = f32(L0_ffn_bg).reshape(1, NEXP)
    w1 = f32(L0_ffn_W1)
    b1 = f32(L0_ffn_b1)
    w2 = f32(L0_ffn_W2)
    b2 = f32(L0_ffn_b2)

    nc1 = _get("L1", build_L1)
    im = [dict(x=x[c * TOK:(c + 1) * TOK], ctx=ctx, wq=wq, wk=wk, wv=wv, wo=wo,
               bout=bout, wg=wg, bg=bg) for c in _cores()]
    r1 = run_bass_kernel_spmd(nc1, im, core_ids=_cores()).results
    s1 = [r["s1"] for r in r1]
    rows_all = np.concatenate([r["rows"] for r in r1], axis=0)
    idx_all = np.concatenate([r["idx"] for r in r1], axis=1)
    lpos_all = np.concatenate([r["lpos"] for r in r1], axis=1)
    gate_all = np.concatenate([r["gate"] for r in r1], axis=1)
    cnt_all = np.concatenate([r["cnt"] for r in r1], axis=0)

    l8 = np.tril(np.ones((8, 8), np.float32), -1).T.copy()
    nc2 = _get("L2", build_L2)

    def ffn_launch(rows_a, idx_a, lpos_a, cnt_a):
        ims = []
        for c in _cores():
            es = slice(c * EXP_PC, (c + 1) * EXP_PC)
            b1c = b1[es].reshape(EXP_PC, HID // 128, 128).transpose(0, 2, 1).copy()
            ims.append(dict(rows_all=rows_a, idx_all=idx_a, lpos_all=lpos_a,
                            cnt_all=cnt_a, w1=w1[es].copy(), w2=w2[es].copy(),
                            b1=b1c, b2=b2[es][:, None, :].copy(), l8=l8,
                            mybase=np.full((128, 1), c * SLOTS, np.float32)))
        rr = run_bass_kernel_spmd(nc2, ims, core_ids=_cores()).results
        y_all = np.concatenate([r["y_slice"] for r in rr], axis=0)
        dest_all = rr[0]["dest_all"]
        return y_all, dest_all

    y1_all, dest1_all = ffn_launch(rows_all, idx_all, lpos_all, cnt_all)
    _DBG.update(s1=s1, rows_all=rows_all, idx_all=idx_all, lpos_all=lpos_all,
                gate_all=gate_all, cnt_all=cnt_all, y1_all=y1_all,
                dest1_all=dest1_all)

    nc3 = _get("L3", build_L3)
    in_w = f32(L0_attn_b_in_w)
    in_b_pc = f32(L0_attn_b_in_b).reshape(3 * CIN // 128, 128).T.copy()
    convw_pc = f32(L0_attn_b_conv_w)[:, 0, :].reshape(CIN // 128, 128, 3)\
        .transpose(1, 0, 2).copy()
    convb_pc = f32(L0_attn_b_conv_b).reshape(CIN // 128, 128).T.copy()
    alog_pc = f32(L0_attn_b_A_log).reshape(CIN // 128, 128, DST)\
        .transpose(1, 0, 2).reshape(128, -1).copy()
    bp_pc = f32(L0_attn_b_B_p).reshape(CIN // 128, 128, DST)\
        .transpose(1, 0, 2).reshape(128, -1).copy()
    cp_pc = f32(L0_attn_b_C_p).reshape(CIN // 128, 128, DST)\
        .transpose(1, 0, 2).reshape(128, -1).copy()
    dp_pc = f32(L0_attn_b_D_p).reshape(CIN // 128, 128).T.copy()
    out_w = f32(L0_attn_b_out_w)
    out_b = f32(L0_attn_b_out_b).reshape(1, D)
    gam = f32(L0_attn_b_norm_gamma).reshape(1, D)
    bet = f32(L0_attn_b_norm_beta).reshape(1, D)
    ims = []
    for c in _cores():
        cs = slice(c * NT, (c + 1) * NT)
        ims.append(dict(s1=s1[c], y_all=y1_all, dest_mine=dest1_all[:, cs].copy(),
                        gate_mine=gate_all[:, cs].copy(), in_w=in_w, in_b=in_b_pc,
                        conv_w=convw_pc, conv_b=convb_pc, a_log=alog_pc, b_p=bp_pc,
                        c_p=cp_pc, d_p=dp_pc, out_w=out_w, out_b=out_b, gam=gam,
                        bet=bet, wg=wg, bg=bg))
    r3 = run_bass_kernel_spmd(nc3, ims, core_ids=_cores()).results
    s3 = [r["s3"] for r in r3]
    rows2_all = np.concatenate([r["rows2"] for r in r3], axis=0)
    idx2_all = np.concatenate([r["idx2"] for r in r3], axis=1)
    lpos2_all = np.concatenate([r["lpos2"] for r in r3], axis=1)
    gate2_all = np.concatenate([r["gate2"] for r in r3], axis=1)
    cnt2_all = np.concatenate([r["cnt2"] for r in r3], axis=0)

    _DBG.update(s3=s3, rows2_all=rows2_all, idx2_all=idx2_all,
                lpos2_all=lpos2_all, gate2_all=gate2_all, cnt2_all=cnt2_all)
    y2_all, dest2_all = ffn_launch(rows2_all, idx2_all, lpos2_all, cnt2_all)
    _DBG.update(y2_all=y2_all, dest2_all=dest2_all)

    nc5 = _get("L5", build_L5)
    ims = []
    for c in _cores():
        cs = slice(c * NT, (c + 1) * NT)
        ims.append(dict(s3=s3[c], y2_all=y2_all, dest2_mine=dest2_all[:, cs].copy(),
                        gate2_mine=gate2_all[:, cs].copy()))
    r5 = run_bass_kernel_spmd(nc5, ims, core_ids=_cores()).results
    out = np.concatenate([r["final"] for r in r5], axis=0)
    return out.reshape(BATCH, SEQ, D)
